# version 12
# baseline (speedup 1.0000x reference)
import numpy as np
from contextlib import ExitStack

import concourse.bass as bass
import concourse.mybir as mybir
from concourse.bass_utils import run_bass_kernel_spmd

F32 = mybir.dt.float32
BF16 = mybir.dt.bfloat16
AF = mybir.ActivationFunctionType
ALU = mybir.AluOpType
AX = mybir.AxisListType

D = 1024
H = 16
DC = 8
EXPERTS = 8
FF = 2048
FC = 16
EXP_BIAS = -20.0

ENGS = ("sync", "tensor", "scalar", "vector", "gpsimd")
DMA_SEMS = ("dx0", "dx1", "dx2", "dx3", "do0", "do1", "dow0", "dow1", "dai", "dwg0", "dwg1", "ddw", "dpre", "dpc0", "dpc1", "dsc0", "dsc1")


class Plan:
    def __init__(self):
        self.ops = {e: [] for e in ENGS}
        self.cnt = {}

    def add(self, eng, fn):
        self.ops[eng].append(fn)

    def bump(self, sem, n=1):
        self.cnt[sem] = self.cnt.get(sem, 0) + n
        return self.cnt[sem]

    def cur(self, sem):
        return self.cnt.get(sem, 0)


def build_program(cfg):
    S = cfg["S"]
    Q = cfg["Q"]
    Tq = Q // 128
    NSUP = S // 512
    NTH = cfg["NTH"]
    TOKTH = Q // NTH
    TTH = TOKTH // 128
    NBLK = Q // 512
    PH = cfg.get("ph", 3)

    nc = bass.Bass("TRN2", target_bir_lowering=False)

    d_xb = nc.dram_tensor("x_batch", [S, D], F32, kind="ExternalInput").ap()
    d_xo = nc.dram_tensor("x_own", [Q, D], F32, kind="ExternalInput").ap()
    d_wq = nc.dram_tensor("wq", [128, DC * D], F32, kind="ExternalInput").ap()
    d_wk = nc.dram_tensor("wk", [128, DC * D], F32, kind="ExternalInput").ap()
    d_wv = nc.dram_tensor("wv", [128, DC * D], F32, kind="ExternalInput").ap()
    d_wo = nc.dram_tensor("wo", [128, DC * D], F32, kind="ExternalInput").ap()
    d_bqr = nc.dram_tensor("bq_r", [128, DC], F32, kind="ExternalInput").ap()
    d_bkr = nc.dram_tensor("bk_r", [128, DC], F32, kind="ExternalInput").ap()
    d_bvbo = nc.dram_tensor("bvbo", [1, 2 * D], F32, kind="ExternalInput").ap()
    d_gw = nc.dram_tensor("gatew_r", [128, DC * EXPERTS], F32, kind="ExternalInput").ap()
    d_gam = nc.dram_tensor("gamma_rep", [128, D], F32, kind="ExternalInput").ap()
    d_bet = nc.dram_tensor("beta_rep", [128, D], F32, kind="ExternalInput").ap()
    d_gup = nc.dram_tensor("gate_up", [EXPERTS, 128, 32, DC * 128], F32, kind="ExternalInput").ap()
    d_wsc = nc.dram_tensor("wsc", [EXPERTS, 128, 48, D], BF16, kind="Internal").ap()
    d_dwn = nc.dram_tensor("down", [EXPERTS, 128, FC * D], F32, kind="ExternalInput").ap()
    d_idn = nc.dram_tensor("ident", [128, 128], F32, kind="ExternalInput").ap()
    d_one = nc.dram_tensor("ones", [1, 128], F32, kind="ExternalInput").ap()
    d_out = nc.dram_tensor("out", [Q, D], F32, kind="ExternalOutput").ap()

    with ExitStack() as ctx:
        en = ctx.enter_context

        xst = en(nc.sbuf_tensor("xst", [128, 4 * D], F32))
        xnb2 = en(nc.sbuf_tensor("xnb2", [128, 2 * D], BF16))
        stats2 = en(nc.sbuf_tensor("stats2", [128, 24], F32))
        mv2 = en(nc.sbuf_tensor("mv2", [128, 4], F32))
        rstd2 = en(nc.sbuf_tensor("rstd2", [128, 2], F32))
        lnt2 = en(nc.sbuf_tensor("lnt2", [128, 2], F32))
        xnb = en(nc.sbuf_tensor("xnb", [128, D], BF16))
        ctxb = en(nc.sbuf_tensor("ctxb", [128, D], BF16))
        stats = en(nc.sbuf_tensor("stats", [128, 12], F32))
        mv = en(nc.sbuf_tensor("mv", [128, 2], F32))
        rstd = en(nc.sbuf_tensor("rstd", [128, 1], F32))
        lnt = en(nc.sbuf_tensor("lnt", [128, 1], F32))
        routw = en(nc.sbuf_tensor("routw", [128, Tq * EXPERTS], F32))
        bqr = en(nc.sbuf_tensor("bqr", [128, DC], F32))
        bkr = en(nc.sbuf_tensor("bkr", [128, DC], F32))
        bvbo = en(nc.sbuf_tensor("bvbo_s", [1, 2 * D], BF16))
        onesb = en(nc.sbuf_tensor("onesb", [1, 128], BF16))
        idn = en(nc.sbuf_tensor("idn", [128, 128], BF16))
        gwb = en(nc.sbuf_tensor("gwb", [128, DC * EXPERTS], BF16))
        rden = en(nc.sbuf_tensor("rden", [128, H], F32))
        el8 = en(nc.sbuf_tensor("el8", [128, EXPERTS], F32))
        top8 = en(nc.sbuf_tensor("top8", [128, EXPERTS], F32))
        tmp8 = en(nc.sbuf_tensor("tmp8", [128, EXPERTS], F32))
        uns8 = en(nc.sbuf_tensor("uns8", [128, EXPERTS], F32))
        den1 = en(nc.sbuf_tensor("den1", [128, 1], F32))
        negm = en(nc.sbuf_tensor("negm", [128, 1], F32))
        ptb = en(nc.sbuf_tensor("ptb", [128, 4 * 1024], BF16))
        constv = en(nc.sbuf_tensor("constv", [128, 3], F32))

        arena_specs = [
            ("ctxa", [128, Tq * H * 65], F32, {1, 2}),
            ("qtb",  [128, DC * Q], BF16, {0, 1}),
            ("xntf", [128, DC * Q], BF16, {2, 3}),
            ("wkb",  [128, DC * D], BF16, {0, 1}),
            ("wvb",  [128, DC * D], BF16, {0, 1}),
            ("wqb",  [128, DC * D], BF16, {0}),
            ("xntb", [128, DC * 512], BF16, {0}),
            ("ktb",  [128, DC * 512], BF16, {1}),
            ("vxb",  [128, 4 * H * 65], BF16, {1}),
            ("xnkb", [128, 4 * D], BF16, {1}),
            ("wob",  [128, DC * D], BF16, {2}),
            ("ctxtb", [128, D], BF16, {2}),
            ("gamr", [128, D], F32, {2}),
            ("betr", [128, D], F32, {2}),
            ("xst2", [128, 2 * D], F32, {2}),
            ("xnf",  [128, D], F32, {2}),
            ("yf",   [128, D], F32, {2}),
            ("outst", [128, 2 * D], F32, {2}),
            ("hidb", [128, FC * TOKTH], BF16, {3}),
            ("pstg", [128, 8 * D], BF16, {1, 2}),
            ("dwb",  [128, FC * D], BF16, {3}),
            ("accb", [128, TTH * D], F32, {3}),
            ("silb", [128, 1024], F32, {3}),
            ("wgbuf", [128, 2 * D], BF16, {3}),
            ("wubuf", [128, 2 * D], BF16, {3}),
        ]
        nbytes = {F32: 4, BF16: 2}
        placed = []
        offs = {}
        for name, shp, dt, ph in arena_specs:
            sz = ((shp[1] * nbytes[dt]) + 31) & ~31
            off = 0
            while True:
                conflict = None
                for o2, s2, p2 in placed:
                    if (ph & p2) and off < o2 + s2 and o2 < off + sz:
                        conflict = o2 + s2
                        break
                if conflict is None:
                    break
                off = (conflict + 31) & ~31
            placed.append((off, sz, ph))
            offs[name] = off
        arena_size = max(o + s for o, s, _ in placed)
        arena = en(nc.sbuf_tensor("arena", [128, arena_size], mybir.dt.uint8))
        abase = nc.lookup_mls(arena).memorylocations[0].addr
        ar = {}
        for name, shp, dt, ph in arena_specs:
            ar[name] = nc.alloc_sbuf_tensor_at(name, shp, dt, offset=abase + offs[name])
        wqb, xntb, qtb = ar["wqb"], ar["xntb"], ar["qtb"]
        ktb, vxb, xnkb, ctxa = ar["ktb"], ar["vxb"], ar["xnkb"], ar["ctxa"]
        wob, ctxtb, gamr, betr = ar["wob"], ar["ctxtb"], ar["gamr"], ar["betr"]
        xntf, hidb, dwb, accb = ar["xntf"], ar["hidb"], ar["dwb"], ar["accb"]
        pstg = ar["pstg"]
        wkb, wvb, xst2, xnf = ar["wkb"], ar["wvb"], ar["xst2"], ar["xnf"]
        yf, outst, silb = ar["yf"], ar["outst"], ar["silb"]
        wgbuf, wubuf = ar["wgbuf"], ar["wubuf"]

        psA = en(nc.psum_tensor("psA", [128, 2048], F32))
        psB = en(nc.psum_tensor("psB", [128, 1536], F32))
        psT = en(nc.psum_tensor("psT", [128, 512], F32))

        sems = {}
        for s in DMA_SEMS + ("tT", "tS", "tV"):
            sems[s] = en(nc.semaphore(s))

        plan = Plan()

        def W(eng, sem, val):
            if val is None or val <= 0:
                return
            plan.add(eng, lambda E, s=sems[sem], v=val: E.wait_ge(s, v))

        def I(eng, fn, sem=None):
            if sem is None and eng == "vector":
                sem = "tV"
            elif sem is None and eng == "scalar":
                sem = "tS"
            if sem is None:
                plan.add(eng, fn)
                return None
            v = plan.bump(sem, 16 if sem in DMA_SEMS else 1)
            plan.add(eng, lambda E, f=fn, s=sems[sem],
                     n=(16 if sem in DMA_SEMS else 1): f(E).then_inc(s, n))
            return v

        def gload(dst_ap, src_ap, sem="dpre"):
            return I("gpsimd", lambda E, o=dst_ap, i=src_ap: E.dma_start(out=o, in_=i), sem)

        def emit_ln_stats(src, eps_col):
            I("vector", lambda E, s=src: E.bn_stats(stats[:, 0:6], s[:, 0:512]))
            v1 = I("vector", lambda E, s=src: E.bn_stats(stats[:, 6:12], s[:, 512:1024]), "tV")
            W("vector", "tV", v1)
            v2 = I("vector", lambda E: E.bn_aggr(mv[:], stats[:]), "tV")
            W("scalar", "tV", v2)
            s0 = I("scalar", lambda E, e=eps_col: E.activation(
                lnt[:], mv[:, 1:2], AF.Ln, bias=constv[:, e:e + 1]), "tS")
            W("scalar", "tS", s0)
            s1 = I("scalar", lambda E: E.activation(rstd[:], lnt[:], AF.Exp, scale=-0.5), "tS")
            return s1

        def emit_normalize(src, out, wait_tT=None):
            if wait_tT is not None:
                W("vector", "tT", wait_tT)
            return I("vector", lambda E, s=src, o=out: E.tensor_scalar(
                out=o, in0=s, scalar1=mv[:, 0:1], scalar2=rstd[:],
                op0=ALU.subtract, op1=ALU.mult), "tV")

        psT_free = [None]

        def emit_transpose8(src_bf, evac_out_3d, wait_tV=None):
            tt = ev = None
            for r in range(2):
                W("tensor", "tV", psT_free[0])
                if r == 0 and wait_tV:
                    W("tensor", "tV", wait_tV)
                for i in range(4):
                    dch = r * 4 + i
                    tt = I("tensor", lambda E, d=dch, i=i, s=src_bf: E.matmul(
                        psT[:, i * 128:(i + 1) * 128],
                        s[:, d * 128:(d + 1) * 128], idn[:],
                        start=(i == 0), stop=(i == 3)),
                        "tT" if i == 3 else None)
                W("vector", "tT", tt)
                ev = I("vector", lambda E, r=r, o=evac_out_3d: E.tensor_copy(
                    o[:, r * 4:(r + 1) * 4, :],
                    psT[:].rearrange("p (c n) -> p c n", c=4)), "tV")
                psT_free[0] = ev
            return tt, ev

        gload(wqb[:], d_wq)
        gload(wkb[:], d_wk)
        gload(wvb[:], d_wv)
        gload(gwb[:], d_gw)
        gload(bvbo[:], d_bvbo)
        gload(onesb[:], d_one)
        gload(idn[:], d_idn)
        gload(bqr[:], d_bqr)
        v_pre = gload(bkr[:], d_bkr)

        I("vector", lambda E: E.memset(constv[:, 0:1], 1e-5))
        I("vector", lambda E: E.memset(constv[:, 1:2], 1e-3))
        v_cst = I("vector", lambda E: E.memset(constv[:, 2:3], EXP_BIAS), "tV")
        W("vector", "dpre", v_pre)
        W("tensor", "dpre", v_pre)
        W("scalar", "dpre", v_pre)
        W("scalar", "tV", v_cst)

        ln_done = {}
        tr_done = {}
        evac_done = {}
        rexp = {}
        qt_evac_slot = {0: None, 1: None}
        for blk in range(NBLK):
            for t4 in range(4):
                t = blk * 4 + t4
                if t >= 4:
                    W("sync", "tV", ln_done[t - 4])
                vdx = I("sync", lambda E, t=t: E.dma_start(
                    out=xst[:, (t % 4) * D:(t % 4 + 1) * D],
                    in_=d_xo[t * 128:(t + 1) * 128, :]), "dx%d" % (t % 4))
                W("vector", "dx%d" % (t % 4), vdx)
                xsrc = xst[:, (t % 4) * D:(t % 4 + 1) * D]
                sr = emit_ln_stats(xsrc, 0)
                W("vector", "tS", sr)
                ln_done[t] = emit_normalize(xsrc, xnb[:], wait_tT=tr_done.get(t - 1))
                dst = xntb[:].rearrange("p (c n) -> p c n", c=DC)[:, :, t4 * 128:(t4 + 1) * 128]
                tr_done[t], evac_done[t] = emit_transpose8(
                    xnb[:], dst, wait_tV=ln_done[t])
                W("tensor", "tV", evac_done[t])
                if t >= 1:
                    W("tensor", "tS", rexp[t - 1])
                rmm = None
                for dch in range(DC):
                    rmm = I("tensor", lambda E, d=dch, t4=t4: E.matmul(
                        psB[:, 0:EXPERTS],
                        xntb[:, d * 512 + t4 * 128:d * 512 + t4 * 128 + 128],
                        gwb[:, d * EXPERTS:(d + 1) * EXPERTS],
                        start=(d == 0), stop=(d == DC - 1)),
                        "tT" if dch == DC - 1 else None)
                W("vector", "tT", rmm)
                v = I("vector", lambda E: E.tensor_reduce(
                    negm[:], psB[:, 0:EXPERTS], AX.X, ALU.max, negate=True), "tV")
                W("scalar", "tV", v)
                rexp[t] = I("scalar", lambda E: E.activation(
                    el8[:], psB[:, 0:EXPERTS], AF.Exp, bias=negm[:]), "tS")
                W("vector", "tS", rexp[t])
                v = I("vector", lambda E: E.max(out=top8[:], in_=el8[:]), "tV")
                W("vector", "tV", v)
                I("vector", lambda E: E.tensor_copy(tmp8[:, 0:2], top8[:, 0:2]))
                v = I("vector", lambda E: E.memset(tmp8[:, 2:EXPERTS], -1e30), "tV")
                W("vector", "tV", v)
                v = I("vector", lambda E: E.match_replace(
                    out=uns8[:], in_to_replace=tmp8[:], in_values=el8[:],
                    imm_value=0.0), "tV")
                W("vector", "tV", v)
                I("vector", lambda E: E.tensor_sub(uns8[:], el8[:], uns8[:]))
                v = I("vector", lambda E: E.tensor_add(den1[:], top8[:, 0:1], top8[:, 1:2]), "tV")
                W("vector", "tV", v)
                v = I("vector", lambda E: E.reciprocal(den1[:], den1[:]), "tV")
                W("vector", "tV", v)
                I("vector", lambda E, t=t: E.tensor_scalar(
                    out=routw[:, t * EXPERTS:(t + 1) * EXPERTS], in0=uns8[:],
                    scalar1=den1[:], scalar2=None, op0=ALU.mult), "tV")
            for hkch in range(DC):
                sl = (hkch % 2) * 512
                W("tensor", "tV", qt_evac_slot[hkch % 2])
                mm = None
                for dch in range(DC):
                    mm = I("tensor", lambda E, d=dch, hk=hkch, sl=sl: E.matmul(
                        psA[:, sl:sl + 512],
                        wqb[:, d * D + hk * 128:d * D + hk * 128 + 128],
                        xntb[:, d * 512:(d + 1) * 512],
                        start=(d == 0), stop=(d == DC - 1)),
                        "tT" if dch == DC - 1 else None)
                W("vector", "tT", mm)
                qt_evac_slot[hkch % 2] = I("vector", lambda E, hk=hkch, b=blk, sl=sl: E.tensor_scalar(
                    out=qtb[:, hk * Q + b * 512:hk * Q + b * 512 + 512],
                    in0=psA[:, sl:sl + 512], scalar1=bqr[:, hk:hk + 1],
                    scalar2=None, op0=ALU.add), "tV")

        p0_tS = plan.cur("tS")
        p0_tV = plan.cur("tV")

        if PH < 1:
            return _finish(nc, plan, sems, W)
        NS2 = S // 256
        W("vector", "tS", p0_tS)
        vx_ones = I("vector", lambda E: E.memset(
            vxb[:].rearrange("p (k h o) -> p k h o", k=4, h=H)[:, :, :, 64:65], 1.0), "tV")

        kt_ln = {}
        kt_tr = {}
        exp_cnt = {0: None, 1: None}
        pv_read_done = {0: None, 1: None, 2: None, 3: None}
        attn_last_exp = None
        prev_ctx_evac = None
        kv_ready = {}
        kv_k_ready = {}
        qloop_pv_end = {}

        kv_rstd = {}
        kv_norm = {}
        kv_aggr = {}

        def make_kv_units(sup):
            par = sup % 2
            units = []

            def u_dma(kt):
                def f(kt=kt):
                    kk = sup * 2 + kt
                    g = Tq + kk
                    if g >= 4:
                        prev = kv_norm.get(kk - 4, ln_done.get(g - 4))
                        W("sync", "tV", prev)
                    I("sync", lambda E, kk=kk: E.dma_start(
                        out=xst[:, ((Tq + kk) % 4) * D:((Tq + kk) % 4 + 1) * D],
                        in_=d_xb[kk * 128:(kk + 1) * 128, :]), "dx%d" % ((Tq + kk) % 4))
                return f

            def u_stats(kt):
                def f(kt=kt):
                    kk = sup * 2 + kt
                    p2 = kk % 2
                    W("vector", "dx%d" % ((Tq + kk) % 4),
                      plan.cur("dx%d" % ((Tq + kk) % 4)))
                    W("vector", "tS", kv_rstd.get(kk - 2))
                    xsrc = xst[:, ((Tq + kk) % 4) * D:((Tq + kk) % 4 + 1) * D]
                    I("vector", lambda E, s2=xsrc, p2=p2: E.bn_stats(
                        stats2[:, p2 * 12:p2 * 12 + 6], s2[:, 0:512]))
                    v1 = I("vector", lambda E, s2=xsrc, p2=p2: E.bn_stats(
                        stats2[:, p2 * 12 + 6:p2 * 12 + 12], s2[:, 512:1024]), "tV")
                    W("vector", "tV", v1)
                    kv_aggr[kk] = I("vector", lambda E, p2=p2: E.bn_aggr(
                        mv2[:, p2 * 2:p2 * 2 + 2], stats2[:, p2 * 12:p2 * 12 + 12]), "tV")
                return f

            def u_rstd(kt):
                def f(kt=kt):
                    kk = sup * 2 + kt
                    p2 = kk % 2
                    W("scalar", "tV", kv_aggr[kk])
                    if kk >= 2:
                        W("scalar", "tV", kv_norm.get(kk - 2))
                    s0 = I("scalar", lambda E, p2=p2: E.activation(
                        lnt2[:, p2:p2 + 1], mv2[:, p2 * 2 + 1:p2 * 2 + 2],
                        AF.Ln, bias=constv[:, 0:1]), "tS")
                    W("scalar", "tS", s0)
                    kv_rstd[kk] = I("scalar", lambda E, p2=p2: E.activation(
                        rstd2[:, p2:p2 + 1], lnt2[:, p2:p2 + 1], AF.Exp, scale=-0.5), "tS")
                return f

            def u_norm(kt):
                def f(kt=kt):
                    kk = sup * 2 + kt
                    p2 = kk % 2
                    W("vector", "tS", kv_rstd[kk])
                    W("vector", "tT", kt_tr.get(kk - 2, tr_done.get(Tq - 1)))
                    xsrc = xst[:, ((Tq + kk) % 4) * D:((Tq + kk) % 4 + 1) * D]
                    kv_norm[kk] = I("vector", lambda E, s2=xsrc, p2=p2: E.tensor_scalar(
                        out=xnb2[:, p2 * D:(p2 + 1) * D], in0=s2,
                        scalar1=mv2[:, p2 * 2:p2 * 2 + 1],
                        scalar2=rstd2[:, p2:p2 + 1],
                        op0=ALU.subtract, op1=ALU.mult), "tV")
                return f

            def u_tr(kt, r):
                def f(kt=kt, r=r):
                    kk = sup * 2 + kt
                    p2 = kk % 2
                    dst = xnkb[:].rearrange("p (k c n) -> p k c n", k=4, c=DC)[:, par * 2 + kt]
                    W("tensor", "tV", psT_free[0])
                    if r == 0:
                        W("tensor", "tV", kv_norm[kk])
                    tt = None
                    for i in range(4):
                        dch = r * 4 + i
                        tt = I("tensor", lambda E, d=dch, i=i, p2=p2: E.matmul(
                            psT[:, i * 128:(i + 1) * 128],
                            xnb2[:, p2 * D + d * 128:p2 * D + d * 128 + 128], idn[:],
                            start=(i == 0), stop=(i == 3)),
                            "tT" if i == 3 else None)
                    kt_tr[kk] = tt
                    W("vector", "tT", tt)
                    ev = I("vector", lambda E, r=r, o=dst: E.tensor_copy(
                        o[:, r * 4:(r + 1) * 4, :],
                        psT[:].rearrange("p (c n) -> p c n", c=4)), "tV")
                    psT_free[0] = ev
                return f

            for kt in range(2):
                units.append(u_dma(kt))
                units.append(u_stats(kt))
                units.append(u_rstd(kt))
                units.append(u_norm(kt))
                units.append(u_tr(kt, 0))
                units.append(u_tr(kt, 1))

            def u_k(p):
                def f(p=p):
                    W("tensor", "tV", psT_free[0])
                    tt = None
                    for g in range(2):
                        hk = 2 * p + g
                        for dch in range(DC):
                            rhs = xnkb[:].rearrange(
                                "p (e k c n) -> p c e k n", e=2, k=2, c=DC)[:, dch, par]
                            tt = I("tensor", lambda E, d=dch, hk=hk, g=g, r=rhs: E.matmul(
                                psT[:, g * 256:g * 256 + 256],
                                wkb[:, d * D + hk * 128:d * D + hk * 128 + 128],
                                r, start=(g == 0 and d == 0),
                                stop=(g == 1 and d == DC - 1)),
                                "tT" if (g == 1 and dch == DC - 1) else None)
                    W("vector", "tT", tt)
                    ev = None
                    for g in range(2):
                        hk = 2 * p + g
                        ev = I("vector", lambda E, hk=hk, g=g: E.tensor_scalar(
                            out=ktb[:, par * 2048 + hk * 256:par * 2048 + hk * 256 + 256],
                            in0=psT[:, g * 256:g * 256 + 256],
                            scalar1=bkr[:, hk:hk + 1], scalar2=None, op0=ALU.add), "tV")
                    psT_free[0] = ev
                    kv_k_ready[sup] = ev
                return f

            for p in range(4):
                units.append(u_k(p))

            def u_v(kt, half):
                def f(kt=kt, half=half):
                    W("tensor", "tV", psT_free[0])
                    mm = None
                    for dch in range(DC):
                        I("tensor", lambda E, d=dch, kt=kt, h=half: E.matmul(
                            psT[:],
                            xnkb[:, (par * 2 + kt) * D + d * 128:(par * 2 + kt) * D + d * 128 + 128],
                            wvb[:, d * D + h * 512:d * D + h * 512 + 512],
                            start=(d == 0), stop=False))
                    mm = I("tensor", lambda E, h=half: E.matmul(
                        psT[:], onesb[:], bvbo[0:1, h * 512:(h + 1) * 512],
                        start=False, stop=True), "tT")
                    W("vector", "tT", mm)
                    ev = I("vector", lambda E, kt=kt, h=half: E.tensor_copy(
                        vxb[:].rearrange("p (k h o) -> p k h o", k=4, h=H)
                        [:, par * 2 + kt, h * 8:(h + 1) * 8, 0:64],
                        psT[:].rearrange("p (h o) -> p h o", h=8)), "tV")
                    psT_free[0] = ev
                    kv_ready[sup] = ev
                return f

            for kt in range(2):
                for half in range(2):
                    units.append(u_v(kt, half))
            return units

        def emit_S(gi, sup, qt, sc, hf):
            par = sup % 2
            slot = gi % 2
            W("tensor", "tS", exp_cnt[slot])
            if (qt, sc, hf) == (0, 0, 0):
                W("tensor", "tV", kv_k_ready[sup])
            mm = None
            for i in range(8):
                h = hf * 8 + i
                po = (h % 2) * 64
                chh = h // 2
                co = (i % 2) * 512 + (i // 2) * 128
                mm = I("tensor", lambda E, sl=slot, co=co, i=i, po=po, ch=chh,
                       sc=sc, qt=qt, par=par: E.matmul(
                    psA[:, sl * 1024 + co:sl * 1024 + co + 128],
                    ktb[po:po + 64, par * 2048 + ch * 256 + sc * 128:
                        par * 2048 + ch * 256 + sc * 128 + 128],
                    qtb[po:po + 64, ch * Q + qt * 128:ch * Q + qt * 128 + 128],
                    start=(i < 2), stop=(i >= 6),
                    tile_position=(po, 0)),
                    "tT" if i == 7 else None)
            return mm

        def emit_exp(gi, smm):
            slot = gi % 2
            pslot = gi % 4
            W("scalar", "tT", smm)
            W("scalar", "tT", pv_read_done[pslot])
            e = I("scalar", lambda E, sl=slot, pl=pslot: E.activation(
                ptb[:, pl * 1024:(pl + 1) * 1024],
                psA[:, sl * 1024:(sl + 1) * 1024],
                AF.Exp, bias=constv[:, 2:3], scale=0.125), "tS")
            exp_cnt[slot] = e
            return e

        def emit_PV(gi, sup, qt, sc, hf, first_of_qt, expv):
            par = sup % 2
            slot = gi % 2
            pslot = gi % 4
            W("tensor", "tS", expv)
            if first_of_qt and prev_ctx_evac_l[0] is not None:
                W("tensor", "tV", prev_ctx_evac_l[0])
            if (qt, sc, hf) == (0, 0, 0):
                W("tensor", "tV", kv_ready[sup])
            mm = None
            for i in range(8):
                h = hf * 8 + i
                co = (h // 7) * 512 + (h % 7) * 65 if h < 14 else 1024 + (h - 14) * 65
                pcol = (i % 2) * 512 + (i // 2) * 128
                st = sc == 0 and h in (0, 7, 14)
                mm = I("tensor", lambda E, pl=pslot, pc=pcol, h=h, co=co, st=st,
                       sc=sc, par=par: E.matmul(
                    psB[:, co:co + 65],
                    ptb[:, pl * 1024 + pc:pl * 1024 + pc + 128],
                    vxb[:, (par * 2 + sc) * H * 65 + h * 65:
                        (par * 2 + sc) * H * 65 + h * 65 + 65],
                    start=st, stop=(sc == 1 and h in (6, 13, 15))),
                    "tT" if i == 7 else None)
            pv_read_done[pslot] = mm
            return mm

        def emit_ctx_evac(qt, last_pv, first_sup):
            W("vector", "tT", last_pv)
            vv = None
            for b in range(3):
                span = 455 if b < 2 else 130
                src = psB[:, b * 512:b * 512 + span]
                dstp = ctxa[:, qt * 1040 + b * 455:qt * 1040 + b * 455 + span]
                if first_sup:
                    vv = I("vector", lambda E, s=src, d=dstp: E.tensor_copy(d, s),
                           "tV" if b == 2 else None)
                else:
                    vv = I("vector", lambda E, s=src, d=dstp: E.tensor_add(d, d, s),
                           "tV" if b == 2 else None)
            return vv

        SCN = 4
        NPE = -(-48 // SCN)
        NST = NPE * EXPERTS
        pc_w = {0: None, 1: None}
        pc_r = {0: None, 1: None}

        def _pc_stage(k):
            e, sub = k // NPE, k % NPE
            hp = k % 2
            dc0 = sub * SCN
            ncn = min(SCN, 48 - dc0)
            if dc0 + ncn <= 32:
                src = d_gup[e, :, dc0:dc0 + ncn, :]
            elif dc0 >= 32:
                src = d_dwn[e].rearrange("p (c n) -> p c n", c=FC)[:, dc0 - 32:dc0 - 32 + ncn, :]
            else:
                raise AssertionError("stage straddles gu/dw boundary")
            stg = pstg[:, hp * SCN * D:hp * SCN * D + ncn * D].rearrange(
                "p (c n) -> p c n", c=ncn)
            return e, hp, dc0, ncn, src, stg

        def make_pc_read(k):
            def f(k=k):
                e, hp, dc0, ncn, src, stg = _pc_stage(k)
                W("gpsimd", "dsc%d" % hp, pc_w[hp])
                pc_r[hp] = I("gpsimd", lambda E, s2=src, d2=stg: E.dma_start(out=d2, in_=s2),
                             "dpc%d" % hp)
            return f

        def make_pc_write(k):
            def f(k=k):
                e, hp, dc0, ncn, src, stg = _pc_stage(k)
                W("scalar", "dpc%d" % hp, pc_r[hp])
                pc_w[hp] = I("scalar", lambda E, e=e, dc0=dc0, d2=stg, ncn=ncn: E.dma_start(
                    out=d_wsc[e, :, dc0:dc0 + ncn, :], in_=d2), "dsc%d" % hp)
            return f

        precast_units = []
        for k in range(NST):
            if k >= 2:
                precast_units.append(make_pc_write(k - 2))
            precast_units.append(make_pc_read(k))
        precast_units.append(make_pc_write(NST - 2))
        precast_units.append(make_pc_write(NST - 1))
        NPC = len(precast_units)
        pc_pos = 0

        prev_ctx_evac_l = [None]
        for u in make_kv_units(0):
            u()
        gidx = 0
        for sup in range(NS2):
            feeder = list(make_kv_units(sup + 1)) if sup + 1 < NS2 else []
            pc_want = NPC * (sup + 1) // NS2
            while pc_pos < pc_want:
                feeder.append(precast_units[pc_pos])
                pc_pos += 1
            fpos = 0
            groups = [(qt, sc, hf) for qt in range(Tq) for sc in range(2) for hf in range(2)]
            pending = []
            def _flush_pv():
                pgi, pqt, psc, phf, pexp = pending.pop(0)
                pv = emit_PV(pgi, sup, pqt, psc, phf,
                             first_of_qt=(psc == 0 and phf == 0), expv=pexp)
                if psc == 1 and phf == 1:
                    prev_ctx_evac_l[0] = emit_ctx_evac(pqt, pv, sup == 0)
                return pv
            for gli, (qt, sc, hf) in enumerate(groups):
                smm = emit_S(gidx, sup, qt, sc, hf)
                attn_last_exp = emit_exp(gidx, smm)
                if len(pending) >= 2:
                    _flush_pv()
                pending.append((gidx, qt, sc, hf, attn_last_exp))
                gidx += 1
                eff = max(1, (len(groups) * 95) // 100)
                want = min(len(feeder), (gli + 1) * len(feeder) // eff)
                while fpos < want:
                    feeder[fpos]()
                    fpos += 1
            while pending:
                pv = _flush_pv()
            qloop_pv_end[sup] = pv
            while fpos < len(feeder):
                feeder[fpos]()
                fpos += 1
        while pc_pos < NPC:
            precast_units[pc_pos]()
            pc_pos += 1
        prev_ctx_evac = prev_ctx_evac_l[0]

        attn_tT = plan.cur("tT")
        attn_tS = plan.cur("tS")
        attn_tV = plan.cur("tV")

        if PH < 2:
            return _finish(nc, plan, sems, W)
        W("gpsimd", "tT", attn_tT)
        W("gpsimd", "tS", attn_tS)
        W("gpsimd", "tV", attn_tV)
        gload(wob[:], d_wo)
        gload(gamr[:], d_gam)
        v_p15 = gload(betr[:], d_bet)

        W("tensor", "dpre", v_p15)
        W("vector", "dpre", v_p15)
        W("vector", "tT", attn_tT)
        W("scalar", "tT", attn_tT)
        ctxT_trans = None
        ctxT_evac = None
        xnT_trans = None
        xnT_evac = None
        xuse_done = {}
        yadd_done = {}
        outw = {}
        for qt in range(Tq):
            W("scalar", "tV", xuse_done.get(qt - 2))
            vdo = I("scalar", lambda E, qt=qt: E.dma_start(
                out=xst2[:, (qt % 2) * D:(qt % 2 + 1) * D],
                in_=d_xo[qt * 128:(qt + 1) * 128, :]), "do%d" % (qt % 2))
            if qt == 0:
                W("vector", "tS", attn_tS)
            v = I("vector", lambda E, qt=qt: E.reciprocal(
                rden[:], ctxa[:].rearrange("p (q h o) -> p q h o", q=Tq, h=H)[:, qt, :, 64]),
                "tV")
            W("vector", "tV", v)
            W("vector", "tT", ctxT_trans)
            vv = None
            for h in range(H):
                vv = I("vector", lambda E, qt=qt, h=h: E.tensor_scalar(
                    out=ctxb[:, h * 64:(h + 1) * 64],
                    in0=ctxa[:, qt * 1040 + h * 65:qt * 1040 + h * 65 + 64],
                    scalar1=rden[:, h:h + 1], scalar2=None, op0=ALU.mult),
                    "tV" if h == H - 1 else None)
            ctxT_trans, ctxT_evac = emit_transpose8(
                ctxb[:], ctxtb[:].rearrange("p (c n) -> p c n", c=DC), wait_tV=vv)
            W("tensor", "tV", ctxT_evac)
            if qt > 0:
                W("tensor", "tV", yadd_done[qt - 1])
            else:
                W("tensor", "tS", attn_last_exp)
            mm = None
            for half in range(2):
                for hkch in range(DC):
                    I("tensor", lambda E, hk=hkch, h=half: E.matmul(
                        psA[:, h * 512:(h + 1) * 512],
                        ctxtb[:, hk * 128:(hk + 1) * 128],
                        wob[:, hk * D + h * 512:hk * D + h * 512 + 512],
                        start=(hk == 0), stop=False))
                mm = I("tensor", lambda E, h=half: E.matmul(
                    psA[:, h * 512:(h + 1) * 512], onesb[:],
                    bvbo[0:1, D + h * 512:D + h * 512 + 512],
                    start=False, stop=True), "tT" if half == 1 else None)
            wo_done = mm
            W("vector", "do%d" % (qt % 2), vdo)
            xsrc = xst2[:, (qt % 2) * D:(qt % 2 + 1) * D]
            sr = emit_ln_stats(xsrc, 0)
            W("vector", "tS", sr)
            W("vector", "tT", xnT_trans)
            emit_normalize(xsrc, xnf[:])
            v = emit_normalize(xsrc, xnb[:])
            xn_ready = v
            dst = xntf[:].rearrange("p (c n) -> p c n", c=DC)[:, :, qt * 128:(qt + 1) * 128]
            xnT_trans, xnT_evac = emit_transpose8(xnb[:], dst, wait_tV=v)
            W("vector", "tT", wo_done)
            W("vector", "tV", xn_ready)
            W("vector", "dow%d" % (qt % 2), outw.get(qt - 2))
            v = I("vector", lambda E: E.tensor_add(yf[:], xnf[:], psA[:, 0:1024]), "tV")
            yadd_done[qt] = v
            W("vector", "tV", v)
            sr = emit_ln_stats(yf[:], 1)
            W("vector", "tS", sr)
            v = I("vector", lambda E: E.tensor_scalar(
                out=yf[:], in0=yf[:], scalar1=mv[:, 0:1], scalar2=rstd[:],
                op0=ALU.subtract, op1=ALU.mult), "tV")
            W("vector", "tV", v)
            v = I("vector", lambda E: E.tensor_mul(yf[:], yf[:], gamr[:]), "tV")
            W("vector", "tV", v)
            v = I("vector", lambda E: E.tensor_add(yf[:], yf[:], betr[:]), "tV")
            W("vector", "tV", v)
            xuse_done[qt] = I("vector", lambda E, qt=qt, x_=xsrc: E.tensor_add(
                outst[:, (qt % 2) * D:(qt % 2 + 1) * D], yf[:], x_), "tV")
            W("scalar", "tV", xuse_done[qt])
            outw[qt] = I("scalar", lambda E, qt=qt: E.dma_start(
                out=d_out[qt * 128:(qt + 1) * 128, :],
                in_=outst[:, (qt % 2) * D:(qt % 2 + 1) * D]), "dow%d" % (qt % 2))

        p15_do0 = plan.cur("dow0")
        p15_do1 = plan.cur("dow1")
        p15_tV = plan.cur("tV")
        p15_tS = plan.cur("tS")

        if PH < 3:
            return _finish(nc, plan, sems, W)
        NB = max(1, TOKTH // 512)
        TBW = min(512, TOKTH)
        gu_mm_done = {}
        hid_done = {}
        hid_half = {}
        comb_done = {}
        comb_mm_last = None
        first_dw = True
        W("sync", "tV", p15_tV)
        W("sync", "tS", p15_tS)
        W("sync", "dow0", p15_do0)
        W("sync", "dow1", p15_do1)
        W("sync", "do0", plan.cur("do0"))
        W("sync", "do1", plan.cur("do1"))
        W("sync", "dsc0", plan.cur("dsc0"))
        W("sync", "dsc1", plan.cur("dsc1"))
        W("vector", "dsc0", plan.cur("dsc0"))
        W("vector", "dsc1", plan.cur("dsc1"))
        for th in range(NTH):
            W("scalar", "dow0", plan.cur("dow0"))
            W("scalar", "dow1", plan.cur("dow1"))
            vdo = I("scalar", lambda E, th=th: E.dma_start(
                out=accb[:].rearrange("p (t d) -> p t d", t=TTH),
                in_=d_out[th * TOKTH:(th + 1) * TOKTH, :]
                .rearrange("(t p) d -> p t d", p=128)), "dai")
            W("vector", "dai", vdo)
            for e in range(EXPERTS):
                for fc in range(FC):
                    buf = fc % 2
                    key_prev = (th, e, fc - 2)
                    if fc < 2:
                        if e > 0:
                            key_prev = (th, e - 1, FC - 2 + fc)
                        elif th > 0:
                            key_prev = (th - 1, EXPERTS - 1, FC - 2 + fc)
                        else:
                            key_prev = None
                    if key_prev is not None:
                        W("sync", "tT", gu_mm_done[key_prev])
                    I("sync", lambda E, e=e, fc=fc, buf=buf: E.dma_start(
                        out=wgbuf[:, buf * D:(buf + 1) * D],
                        in_=d_wsc[e, :, fc, :]), "dwg%d" % buf)
                    vg = I("sync", lambda E, e=e, fc=fc, buf=buf: E.dma_start(
                        out=wubuf[:, buf * D:(buf + 1) * D],
                        in_=d_wsc[e, :, FC + fc, :]), "dwg%d" % buf)
                    W("tensor", "dwg%d" % buf, vg)
                    if (th, e, fc) == (0, 0, 0):
                        W("tensor", "tS", p15_tS)
                        W("tensor", "tV", p15_tV)
                    key_ps = (th, e, fc - 1, 0)
                    if fc == 0:
                        if e > 0:
                            key_ps = (th, e - 1, FC - 1, 0)
                        elif th > 0:
                            key_ps = (th - 1, EXPERTS - 1, FC - 1, 0)
                        else:
                            key_ps = None
                    if key_ps is not None and key_ps in hid_half:
                        W("tensor", "tS", hid_half[key_ps][0])
                        W("tensor", "tV", hid_half[key_ps][1])
                    sl_mm = sl_silu = sl_mul = None
                    for nb in range(NB):
                        u = fc * NB + nb
                        sl = u % 2
                        kp2 = (th, e, fc - 1, 1)
                        if nb == 1 and fc == 0:
                            kp2 = (th, e - 1, FC - 1, 1) if e > 0 else (
                                (th - 1, EXPERTS - 1, FC - 1, 1) if th > 0 else None)
                        if nb == 1 and kp2 is not None and kp2 in hid_half:
                            W("tensor", "tS", hid_half[kp2][0])
                            W("tensor", "tV", hid_half[kp2][1])
                        mm = None
                        for gu in range(2):
                            wb = wgbuf if gu == 0 else wubuf
                            for dch in range(DC):
                                mm = I("tensor", lambda E, d=dch, nb=nb, th=th, buf=buf,
                                       sl=sl, gu=gu, w=wb: E.matmul(
                                    psA[:, sl * 1024 + gu * 512:sl * 1024 + gu * 512 + TBW],
                                    w[:, buf * D + d * 128:buf * D + d * 128 + 128],
                                    xntf[:, d * Q + th * TOKTH + nb * TBW:
                                         d * Q + th * TOKTH + nb * TBW + TBW],
                                    start=(d == 0), stop=(d == DC - 1)),
                                    "tT" if (gu == 1 and dch == DC - 1) else None)
                        W("scalar", "tT", mm)
                        sl_silu = I("scalar", lambda E, sl=sl: E.activation(
                            silb[:, sl * 512:sl * 512 + TBW],
                            psA[:, sl * 1024:sl * 1024 + TBW], AF.Silu), "tS")
                        W("vector", "tS", sl_silu)
                        sl_mul = I("vector", lambda E, fc=fc, nb=nb, sl=sl: E.tensor_mul(
                            hidb[:, fc * TOKTH + nb * TBW:fc * TOKTH + nb * TBW + TBW],
                            silb[:, sl * 512:sl * 512 + TBW],
                            psA[:, sl * 1024 + 512:sl * 1024 + 512 + TBW]), "tV")
                        hid_half[(th, e, fc, nb)] = (sl_silu, sl_mul)
                        sl_mm = mm
                    gu_mm_done[(th, e, fc)] = sl_mm
                    hid_done[(th, e, fc)] = (sl_silu, sl_mul)
                if first_dw:
                    first_dw = False
                else:
                    W("sync", "tT", comb_mm_last)
                vg = I("sync", lambda E, e=e: E.dma_start(
                    out=dwb[:], in_=d_wsc[e, :, 32:48, :]), "ddw")
                W("tensor", "ddw", vg)
                W("tensor", "tV", hid_done[(th, e, FC - 1)][1])
                for tt in range(TTH):
                    for dh in range(2):
                        idx = tt * 2 + dh
                        sl = idx % 2
                        key_c = (th, e, idx - 2)
                        if idx < 2:
                            if e > 0:
                                key_c = (th, e - 1, 2 * TTH - 2 + idx)
                            elif th > 0:
                                key_c = (th - 1, EXPERTS - 1, 2 * TTH - 2 + idx)
                            else:
                                key_c = None
                        if key_c is not None and key_c in comb_done:
                            W("tensor", "tV", comb_done[key_c])
                        mm = None
                        for fc in range(FC):
                            mm = I("tensor", lambda E, fc=fc, tt=tt, dh=dh, sl=sl: E.matmul(
                                psB[:, sl * 512:(sl + 1) * 512],
                                hidb[:, fc * TOKTH + tt * 128:fc * TOKTH + tt * 128 + 128],
                                dwb[:, fc * D + dh * 512:fc * D + dh * 512 + 512],
                                start=(fc == 0), stop=(fc == FC - 1)),
                                "tT" if fc == FC - 1 else None)
                        W("vector", "tT", mm)
                        gt = th * TTH + tt
                        comb_done[(th, e, idx)] = I(
                            "vector",
                            lambda E, tt=tt, dh=dh, sl=sl, gt=gt, e=e: E.scalar_tensor_tensor(
                                out=accb[:, tt * D + dh * 512:tt * D + dh * 512 + 512],
                                in0=psB[:, sl * 512:(sl + 1) * 512],
                                scalar=routw[:, gt * EXPERTS + e:gt * EXPERTS + e + 1],
                                in1=accb[:, tt * D + dh * 512:tt * D + dh * 512 + 512],
                                op0=ALU.mult, op1=ALU.add), "tV")
                comb_mm_last = plan.cur("tT")
            W("scalar", "tV", comb_done[(th, EXPERTS - 1, 2 * TTH - 1)])
            I("scalar", lambda E, th=th: E.dma_start(
                out=d_out[th * TOKTH:(th + 1) * TOKTH, :]
                .rearrange("(t p) d -> p t d", p=128),
                in_=accb[:].rearrange("p (t d) -> p t d", t=TTH)), "dow%d" % (th % 2))

        return _finish(nc, plan, sems, W)


def _finish(nc, plan, sems, W):
    for sm in DMA_SEMS:
        W("sync", sm, plan.cur(sm))
    for sm in ("tT", "tS", "tV"):
        W("sync", sm, plan.cur(sm))
    with nc.Block() as block:
        @block.sync
        def _(E):
            for fn in plan.ops["sync"]:
                fn(E)

        @block.tensor
        def _(E):
            for fn in plan.ops["tensor"]:
                fn(E)

        @block.scalar
        def _(E):
            for fn in plan.ops["scalar"]:
                fn(E)

        @block.vector
        def _(E):
            for fn in plan.ops["vector"]:
                fn(E)

        @block.gpsimd
        def _(E):
            for fn in plan.ops["gpsimd"]:
                fn(E)
    return nc


FULL_CFG = dict(S=8192, Q=2048, NTH=2)


def make_in_maps(cfg, n_cores, L0_attn_x, Wq, bq, Wk, bk, Wv, bv, Wo, bo,
                 gamma, beta, gate_w, gate_up_w, down_w):
    S, Q = cfg["S"], cfg["Q"]
    f32 = np.float32
    X = np.asarray(L0_attn_x, f32)
    def _wr(w):
        return np.ascontiguousarray(
            np.asarray(w, f32).reshape(D, D).reshape(DC, 128, D)
            .transpose(1, 0, 2).reshape(128, DC * D))
    wq, wk, wv, wo = _wr(Wq), _wr(Wk), _wr(Wv), _wr(Wo)
    bqr = np.ascontiguousarray(np.asarray(bq, f32).reshape(D).reshape(DC, 128).T)
    bkr = np.ascontiguousarray(np.asarray(bk, f32).reshape(D).reshape(DC, 128).T)
    bvbo = np.ascontiguousarray(np.concatenate(
        [np.asarray(bv, f32).reshape(D), np.asarray(bo, f32).reshape(D)])[None, :])
    gw = np.ascontiguousarray(
        np.asarray(gate_w, f32).reshape(DC, 128, EXPERTS).transpose(1, 0, 2)
        .reshape(128, DC * EXPERTS))
    gam = np.ascontiguousarray(np.broadcast_to(np.asarray(gamma, f32), (128, D)))
    bet = np.ascontiguousarray(np.broadcast_to(np.asarray(beta, f32), (128, D)))
    gup = np.ascontiguousarray(
        np.asarray(gate_up_w, f32).reshape(EXPERTS, DC, 128, 32, 128)
        .transpose(0, 2, 3, 1, 4).reshape(EXPERTS, 128, 32, DC * 128))
    dwn = np.ascontiguousarray(
        np.asarray(down_w, f32).reshape(EXPERTS, FC, 128, D)
        .transpose(0, 2, 1, 3).reshape(EXPERTS, 128, FC * D))
    ident = np.eye(128, dtype=f32)
    ones = np.ones((1, 128), f32)
    n_q = X.shape[1] // Q
    in_maps = []
    for c in range(n_cores):
        b, q = c // n_q, c % n_q
        in_maps.append(dict(
            x_batch=np.ascontiguousarray(X[b, :S]),
            x_own=np.ascontiguousarray(X[b, q * Q:(q + 1) * Q]),
            wq=wq, wk=wk, wv=wv, wo=wo,
            bq_r=bqr, bk_r=bkr, bvbo=bvbo, gatew_r=gw,
            gamma_rep=gam, beta_rep=bet, gate_up=gup, down=dwn,
            ident=ident, ones=ones))
    return in_maps


def kernel(L0_attn_x, L0_attn_Wq, L0_attn_bq, L0_attn_Wk, L0_attn_bk,
           L0_attn_Wv, L0_attn_bv, L0_attn_Wo, L0_attn_bo, L0_attn_gamma,
           L0_attn_beta, L0_ffn_gate_w, L0_ffn_gate_up_w, L0_ffn_down_w):
    cfg = FULL_CFG
    in_maps = make_in_maps(cfg, 8, L0_attn_x, L0_attn_Wq, L0_attn_bq,
                           L0_attn_Wk, L0_attn_bk, L0_attn_Wv, L0_attn_bv,
                           L0_attn_Wo, L0_attn_bo, L0_attn_gamma, L0_attn_beta,
                           L0_ffn_gate_w, L0_ffn_gate_up_w, L0_ffn_down_w)
    nc = build_program(cfg)
    res = run_bass_kernel_spmd(nc, in_maps, core_ids=list(range(8)))
    outs = [np.asarray(r["out"]) for r in res.results]
    Q = cfg["Q"]
    full = np.empty((2, 8192, D), np.float32)
    for c in range(8):
        b, q = c // 4, c % 4
        full[b, q * Q:(q + 1) * Q] = outs[c]
    return full


# version 17
# speedup vs baseline: 1.0307x; 1.0307x over previous
import numpy as np
from contextlib import ExitStack

import concourse.bass as bass
import concourse.mybir as mybir
from concourse.bass_utils import run_bass_kernel_spmd

F32 = mybir.dt.float32
BF16 = mybir.dt.bfloat16
AF = mybir.ActivationFunctionType
ALU = mybir.AluOpType
AX = mybir.AxisListType

D = 1024
H = 16
DC = 8
EXPERTS = 8
FF = 2048
FC = 16
EXP_BIAS = -20.0

ENGS = ("sync", "tensor", "scalar", "vector", "gpsimd")
DMA_SEMS = ("dx0", "dx1", "dx2", "dx3", "do0", "do1", "dow0", "dow1", "dai", "dwg0", "dwg1", "ddw", "dpre", "dpc0", "dpc1", "dsc0", "dsc1")


class Plan:
    def __init__(self):
        self.ops = {e: [] for e in ENGS}
        self.cnt = {}

    def add(self, eng, fn):
        self.ops[eng].append(fn)

    def bump(self, sem, n=1):
        self.cnt[sem] = self.cnt.get(sem, 0) + n
        return self.cnt[sem]

    def cur(self, sem):
        return self.cnt.get(sem, 0)


def build_program(cfg):
    S = cfg["S"]
    Q = cfg["Q"]
    Tq = Q // 128
    NSUP = S // 512
    NTH = cfg["NTH"]
    TOKTH = Q // NTH
    TTH = TOKTH // 128
    NBLK = Q // 512
    PH = cfg.get("ph", 3)

    nc = bass.Bass("TRN2", target_bir_lowering=False)

    d_xb = nc.dram_tensor("x_batch", [S, D], F32, kind="ExternalInput").ap()
    d_xo = nc.dram_tensor("x_own", [Q, D], F32, kind="ExternalInput").ap()
    d_wq = nc.dram_tensor("wq", [128, DC * D], F32, kind="ExternalInput").ap()
    d_wk = nc.dram_tensor("wk", [128, DC * D], F32, kind="ExternalInput").ap()
    d_wv = nc.dram_tensor("wv", [128, DC * D], F32, kind="ExternalInput").ap()
    d_wo = nc.dram_tensor("wo", [128, DC * D], F32, kind="ExternalInput").ap()
    d_bqr = nc.dram_tensor("bq_r", [128, DC], F32, kind="ExternalInput").ap()
    d_bkr = nc.dram_tensor("bk_r", [128, DC], F32, kind="ExternalInput").ap()
    d_bvbo = nc.dram_tensor("bvbo", [1, 2 * D], F32, kind="ExternalInput").ap()
    d_gw = nc.dram_tensor("gatew_r", [128, DC * EXPERTS], F32, kind="ExternalInput").ap()
    d_gam = nc.dram_tensor("gamma_rep", [128, D], F32, kind="ExternalInput").ap()
    d_bet = nc.dram_tensor("beta_rep", [128, D], F32, kind="ExternalInput").ap()
    d_gup = nc.dram_tensor("gate_up", [EXPERTS, 128, 32, DC * 128], F32, kind="ExternalInput").ap()
    d_wsc = nc.dram_tensor("wsc", [EXPERTS, 128, 48, D], BF16, kind="Internal").ap()
    d_dwn = nc.dram_tensor("down", [EXPERTS, 128, FC * D], F32, kind="ExternalInput").ap()
    d_idn = nc.dram_tensor("ident", [128, 128], F32, kind="ExternalInput").ap()
    d_one = nc.dram_tensor("ones", [1, 128], F32, kind="ExternalInput").ap()
    d_out = nc.dram_tensor("out", [Q, D], F32, kind="ExternalOutput").ap()

    with ExitStack() as ctx:
        en = ctx.enter_context

        xst = en(nc.sbuf_tensor("xst", [128, 4 * D], F32))
        xnb2 = en(nc.sbuf_tensor("xnb2", [128, 2 * D], BF16))
        stats2 = en(nc.sbuf_tensor("stats2", [128, 24], F32))
        mv2 = en(nc.sbuf_tensor("mv2", [128, 4], F32))
        rstd2 = en(nc.sbuf_tensor("rstd2", [128, 2], F32))
        lnt2 = en(nc.sbuf_tensor("lnt2", [128, 2], F32))
        xnb = en(nc.sbuf_tensor("xnb", [128, D], BF16))
        stats = en(nc.sbuf_tensor("stats", [128, 12], F32))
        mv = en(nc.sbuf_tensor("mv", [128, 2], F32))
        rstd = en(nc.sbuf_tensor("rstd", [128, 1], F32))
        lnt = en(nc.sbuf_tensor("lnt", [128, 1], F32))
        routw = en(nc.sbuf_tensor("routw", [128, Tq * EXPERTS], F32))
        bqr = en(nc.sbuf_tensor("bqr", [128, DC], F32))
        bkr = en(nc.sbuf_tensor("bkr", [128, DC], F32))
        bvbo = en(nc.sbuf_tensor("bvbo_s", [1, 2 * D], BF16))
        onesb = en(nc.sbuf_tensor("onesb", [1, 128], BF16))
        idn = en(nc.sbuf_tensor("idn", [128, 128], BF16))
        gwb = en(nc.sbuf_tensor("gwb", [128, DC * EXPERTS], BF16))
        rden = en(nc.sbuf_tensor("rden", [128, 2 * H], F32))
        el8 = en(nc.sbuf_tensor("el8", [128, EXPERTS], F32))
        top8 = en(nc.sbuf_tensor("top8", [128, EXPERTS], F32))
        tmp8 = en(nc.sbuf_tensor("tmp8", [128, EXPERTS], F32))
        uns8 = en(nc.sbuf_tensor("uns8", [128, EXPERTS], F32))
        den1 = en(nc.sbuf_tensor("den1", [128, 1], F32))
        negm = en(nc.sbuf_tensor("negm", [128, 1], F32))
        ptb = en(nc.sbuf_tensor("ptb", [128, 4 * 1024], BF16))
        constv = en(nc.sbuf_tensor("constv", [128, 3], F32))

        arena_specs = [
            ("ctxa", [128, Tq * H * 65], F32, {1, 2}),
            ("qtb",  [128, DC * Q], BF16, {0, 1}),
            ("xntf", [128, DC * Q], BF16, {2, 3}),
            ("wkb",  [128, DC * D], BF16, {0, 1}),
            ("wvb",  [128, DC * D], BF16, {0, 1}),
            ("wqb",  [128, DC * D], BF16, {0}),
            ("xntb", [128, DC * 512], BF16, {0}),
            ("ktb",  [128, DC * 512], BF16, {1}),
            ("vxb",  [128, 4 * H * 65], BF16, {1}),
            ("xnkb", [128, 4 * D], BF16, {1}),
            ("wob",  [128, DC * D], BF16, {2}),
            ("ctxtb", [128, 2 * D], BF16, {2}),
            ("ctxb", [128, 2 * D], BF16, {2}),
            ("gamr", [128, D], F32, {2}),
            ("betr", [128, D], F32, {2}),
            ("xst2", [128, 2 * D], F32, {2}),
            ("xnf",  [128, 2 * D], F32, {2}),
            ("yf",   [128, D], F32, {2}),
            ("outst", [128, 2 * D], F32, {2}),
            ("hidb", [128, FC * TOKTH], BF16, {3}),
            ("pstg", [128, 4 * D], BF16, {1, 2}),
            ("dwb",  [128, FC * D], BF16, {3}),
            ("accb", [128, TTH * D], F32, {3}),
            ("silb", [128, 1024], F32, {3}),
            ("wgbuf", [128, 2 * D], BF16, {3}),
            ("wubuf", [128, 2 * D], BF16, {3}),
        ]
        nbytes = {F32: 4, BF16: 2}
        placed = []
        offs = {}
        arena_specs = sorted(arena_specs,
                             key=lambda x: (-len(x[3]), -x[1][1] * nbytes[x[2]]))
        for name, shp, dt, ph in arena_specs:
            sz = ((shp[1] * nbytes[dt]) + 31) & ~31
            off = 0
            while True:
                conflict = None
                for o2, s2, p2 in placed:
                    if (ph & p2) and off < o2 + s2 and o2 < off + sz:
                        conflict = o2 + s2
                        break
                if conflict is None:
                    break
                off = (conflict + 31) & ~31
            placed.append((off, sz, ph))
            offs[name] = off
        arena_size = max(o + s for o, s, _ in placed)
        arena = en(nc.sbuf_tensor("arena", [128, arena_size], mybir.dt.uint8))
        abase = nc.lookup_mls(arena).memorylocations[0].addr
        ar = {}
        for name, shp, dt, ph in arena_specs:
            ar[name] = nc.alloc_sbuf_tensor_at(name, shp, dt, offset=abase + offs[name])
        wqb, xntb, qtb = ar["wqb"], ar["xntb"], ar["qtb"]
        ktb, vxb, xnkb, ctxa = ar["ktb"], ar["vxb"], ar["xnkb"], ar["ctxa"]
        wob, ctxtb, gamr, betr = ar["wob"], ar["ctxtb"], ar["gamr"], ar["betr"]
        xntf, hidb, dwb, accb = ar["xntf"], ar["hidb"], ar["dwb"], ar["accb"]
        pstg = ar["pstg"]
        ctxb = ar["ctxb"]
        wkb, wvb, xst2, xnf = ar["wkb"], ar["wvb"], ar["xst2"], ar["xnf"]
        yf, outst, silb = ar["yf"], ar["outst"], ar["silb"]
        wgbuf, wubuf = ar["wgbuf"], ar["wubuf"]

        psA = en(nc.psum_tensor("psA", [128, 2048], F32))
        psB = en(nc.psum_tensor("psB", [128, 1536], F32))
        psT = en(nc.psum_tensor("psT", [128, 512], F32))

        sems = {}
        for s in DMA_SEMS + ("tT", "tS", "tV"):
            sems[s] = en(nc.semaphore(s))

        plan = Plan()

        def W(eng, sem, val):
            if val is None or val <= 0:
                return
            plan.add(eng, lambda E, s=sems[sem], v=val: E.wait_ge(s, v))

        def I(eng, fn, sem=None):
            if sem is None and eng == "vector":
                sem = "tV"
            elif sem is None and eng == "scalar":
                sem = "tS"
            if sem is None:
                plan.add(eng, fn)
                return None
            v = plan.bump(sem, 16 if sem in DMA_SEMS else 1)
            plan.add(eng, lambda E, f=fn, s=sems[sem],
                     n=(16 if sem in DMA_SEMS else 1): f(E).then_inc(s, n))
            return v

        def gload(dst_ap, src_ap, sem="dpre"):
            return I("gpsimd", lambda E, o=dst_ap, i=src_ap: E.dma_start(out=o, in_=i), sem)

        def emit_ln_stats(src, eps_col):
            I("vector", lambda E, s=src: E.bn_stats(stats[:, 0:6], s[:, 0:512]))
            v1 = I("vector", lambda E, s=src: E.bn_stats(stats[:, 6:12], s[:, 512:1024]), "tV")
            W("vector", "tV", v1)
            v2 = I("vector", lambda E: E.bn_aggr(mv[:], stats[:]), "tV")
            W("scalar", "tV", v2)
            s0 = I("scalar", lambda E, e=eps_col: E.activation(
                lnt[:], mv[:, 1:2], AF.Ln, bias=constv[:, e:e + 1]), "tS")
            W("scalar", "tS", s0)
            s1 = I("scalar", lambda E: E.activation(rstd[:], lnt[:], AF.Exp, scale=-0.5), "tS")
            return s1

        def emit_normalize(src, out, wait_tT=None):
            if wait_tT is not None:
                W("vector", "tT", wait_tT)
            return I("vector", lambda E, s=src, o=out: E.tensor_scalar(
                out=o, in0=s, scalar1=mv[:, 0:1], scalar2=rstd[:],
                op0=ALU.subtract, op1=ALU.mult), "tV")

        psT_free = [None]

        def emit_transpose8(src_bf, evac_out_3d, wait_tV=None):
            tt = ev = None
            for r in range(2):
                W("tensor", "tV", psT_free[0])
                if r == 0 and wait_tV:
                    W("tensor", "tV", wait_tV)
                for i in range(4):
                    dch = r * 4 + i
                    tt = I("tensor", lambda E, d=dch, i=i, s=src_bf: E.matmul(
                        psT[:, i * 128:(i + 1) * 128],
                        s[:, d * 128:(d + 1) * 128], idn[:],
                        start=(i == 0), stop=(i == 3)),
                        "tT" if i == 3 else None)
                W("vector", "tT", tt)
                ev = I("vector", lambda E, r=r, o=evac_out_3d: E.tensor_copy(
                    o[:, r * 4:(r + 1) * 4, :],
                    psT[:].rearrange("p (c n) -> p c n", c=4)), "tV")
                psT_free[0] = ev
            return tt, ev

        gload(wqb[:], d_wq)
        gload(wkb[:], d_wk)
        gload(wvb[:], d_wv)
        gload(gwb[:], d_gw)
        gload(bvbo[:], d_bvbo)
        gload(onesb[:], d_one)
        gload(idn[:], d_idn)
        gload(bqr[:], d_bqr)
        v_pre = gload(bkr[:], d_bkr)

        I("vector", lambda E: E.memset(constv[:, 0:1], 1e-5))
        I("vector", lambda E: E.memset(constv[:, 1:2], 1e-3))
        v_cst = I("vector", lambda E: E.memset(constv[:, 2:3], EXP_BIAS), "tV")
        W("vector", "dpre", v_pre)
        W("tensor", "dpre", v_pre)
        W("scalar", "dpre", v_pre)
        W("scalar", "tV", v_cst)

        ln_done = {}
        tr_done = {}
        evac_done = {}
        rexp = {}
        qt_evac_slot = {0: None, 1: None}
        for blk in range(NBLK):
            for t4 in range(4):
                t = blk * 4 + t4
                if t >= 4:
                    W("sync", "tV", ln_done[t - 4])
                vdx = I("sync", lambda E, t=t: E.dma_start(
                    out=xst[:, (t % 4) * D:(t % 4 + 1) * D],
                    in_=d_xo[t * 128:(t + 1) * 128, :]), "dx%d" % (t % 4))
                W("vector", "dx%d" % (t % 4), vdx)
                xsrc = xst[:, (t % 4) * D:(t % 4 + 1) * D]
                sr = emit_ln_stats(xsrc, 0)
                W("vector", "tS", sr)
                ln_done[t] = emit_normalize(xsrc, xnb[:], wait_tT=tr_done.get(t - 1))
                dst = xntb[:].rearrange("p (c n) -> p c n", c=DC)[:, :, t4 * 128:(t4 + 1) * 128]
                tr_done[t], evac_done[t] = emit_transpose8(
                    xnb[:], dst, wait_tV=ln_done[t])
                W("tensor", "tV", evac_done[t])
                if t >= 1:
                    W("tensor", "tS", rexp[t - 1])
                rmm = None
                for dch in range(DC):
                    rmm = I("tensor", lambda E, d=dch, t4=t4: E.matmul(
                        psB[:, 0:EXPERTS],
                        xntb[:, d * 512 + t4 * 128:d * 512 + t4 * 128 + 128],
                        gwb[:, d * EXPERTS:(d + 1) * EXPERTS],
                        start=(d == 0), stop=(d == DC - 1)),
                        "tT" if dch == DC - 1 else None)
                W("vector", "tT", rmm)
                v = I("vector", lambda E: E.tensor_reduce(
                    negm[:], psB[:, 0:EXPERTS], AX.X, ALU.max, negate=True), "tV")
                W("scalar", "tV", v)
                rexp[t] = I("scalar", lambda E: E.activation(
                    el8[:], psB[:, 0:EXPERTS], AF.Exp, bias=negm[:]), "tS")
                W("vector", "tS", rexp[t])
                v = I("vector", lambda E: E.max(out=top8[:], in_=el8[:]), "tV")
                W("vector", "tV", v)
                I("vector", lambda E: E.tensor_copy(tmp8[:, 0:2], top8[:, 0:2]))
                v = I("vector", lambda E: E.memset(tmp8[:, 2:EXPERTS], -1e30), "tV")
                W("vector", "tV", v)
                v = I("vector", lambda E: E.match_replace(
                    out=uns8[:], in_to_replace=tmp8[:], in_values=el8[:],
                    imm_value=0.0), "tV")
                W("vector", "tV", v)
                I("vector", lambda E: E.tensor_sub(uns8[:], el8[:], uns8[:]))
                v = I("vector", lambda E: E.tensor_add(den1[:], top8[:, 0:1], top8[:, 1:2]), "tV")
                W("vector", "tV", v)
                v = I("vector", lambda E: E.reciprocal(den1[:], den1[:]), "tV")
                W("vector", "tV", v)
                I("vector", lambda E, t=t: E.tensor_scalar(
                    out=routw[:, t * EXPERTS:(t + 1) * EXPERTS], in0=uns8[:],
                    scalar1=den1[:], scalar2=None, op0=ALU.mult), "tV")
            for hkch in range(DC):
                sl = (hkch % 2) * 512
                W("tensor", "tV", qt_evac_slot[hkch % 2])
                mm = None
                for dch in range(DC):
                    mm = I("tensor", lambda E, d=dch, hk=hkch, sl=sl: E.matmul(
                        psA[:, sl:sl + 512],
                        wqb[:, d * D + hk * 128:d * D + hk * 128 + 128],
                        xntb[:, d * 512:(d + 1) * 512],
                        start=(d == 0), stop=(d == DC - 1)),
                        "tT" if dch == DC - 1 else None)
                W("vector", "tT", mm)
                qt_evac_slot[hkch % 2] = I("vector", lambda E, hk=hkch, b=blk, sl=sl: E.tensor_scalar(
                    out=qtb[:, hk * Q + b * 512:hk * Q + b * 512 + 512],
                    in0=psA[:, sl:sl + 512], scalar1=bqr[:, hk:hk + 1],
                    scalar2=None, op0=ALU.add), "tV")

        p0_tS = plan.cur("tS")
        p0_tV = plan.cur("tV")
        p0_tT = plan.cur("tT")

        if PH < 1:
            return _finish(nc, plan, sems, W)
        NS2 = S // 256
        W("vector", "tS", p0_tS)
        vx_ones = I("vector", lambda E: E.memset(
            vxb[:].rearrange("p (k h o) -> p k h o", k=4, h=H)[:, :, :, 64:65], 1.0), "tV")

        kt_ln = {}
        kt_tr = {}
        exp_cnt = {0: None, 1: None}
        pv_read_done = {0: None, 1: None, 2: None, 3: None}
        attn_last_exp = None
        prev_ctx_evac = None
        kv_ready = {}
        kv_k_ready = {}
        qloop_pv_end = {}

        kv_rstd = {}
        kv_norm = {}
        kv_aggr = {}

        def make_kv_units(sup):
            par = sup % 2
            units = []

            def u_dma(kt):
                def f(kt=kt):
                    kk = sup * 2 + kt
                    g = Tq + kk
                    if g >= 4:
                        prev = kv_norm.get(kk - 4, ln_done.get(g - 4))
                        W("sync", "tV", prev)
                    I("sync", lambda E, kk=kk: E.dma_start(
                        out=xst[:, ((Tq + kk) % 4) * D:((Tq + kk) % 4 + 1) * D],
                        in_=d_xb[kk * 128:(kk + 1) * 128, :]), "dx%d" % ((Tq + kk) % 4))
                return f

            def u_stats(kt):
                def f(kt=kt):
                    kk = sup * 2 + kt
                    p2 = kk % 2
                    W("vector", "dx%d" % ((Tq + kk) % 4),
                      plan.cur("dx%d" % ((Tq + kk) % 4)))
                    W("vector", "tS", kv_rstd.get(kk - 2))
                    xsrc = xst[:, ((Tq + kk) % 4) * D:((Tq + kk) % 4 + 1) * D]
                    I("vector", lambda E, s2=xsrc, p2=p2: E.bn_stats(
                        stats2[:, p2 * 12:p2 * 12 + 6], s2[:, 0:512]))
                    v1 = I("vector", lambda E, s2=xsrc, p2=p2: E.bn_stats(
                        stats2[:, p2 * 12 + 6:p2 * 12 + 12], s2[:, 512:1024]), "tV")
                    W("vector", "tV", v1)
                    kv_aggr[kk] = I("vector", lambda E, p2=p2: E.bn_aggr(
                        mv2[:, p2 * 2:p2 * 2 + 2], stats2[:, p2 * 12:p2 * 12 + 12]), "tV")
                return f

            def u_rstd(kt):
                def f(kt=kt):
                    kk = sup * 2 + kt
                    p2 = kk % 2
                    W("scalar", "tV", kv_aggr[kk])
                    if kk >= 2:
                        W("scalar", "tV", kv_norm.get(kk - 2))
                    s0 = I("scalar", lambda E, p2=p2: E.activation(
                        lnt2[:, p2:p2 + 1], mv2[:, p2 * 2 + 1:p2 * 2 + 2],
                        AF.Ln, bias=constv[:, 0:1]), "tS")
                    W("scalar", "tS", s0)
                    kv_rstd[kk] = I("scalar", lambda E, p2=p2: E.activation(
                        rstd2[:, p2:p2 + 1], lnt2[:, p2:p2 + 1], AF.Exp, scale=-0.5), "tS")
                return f

            def u_norm(kt):
                def f(kt=kt):
                    kk = sup * 2 + kt
                    p2 = kk % 2
                    W("vector", "tS", kv_rstd[kk])
                    W("vector", "tT", kt_tr.get(kk - 2, tr_done.get(Tq - 1)))
                    xsrc = xst[:, ((Tq + kk) % 4) * D:((Tq + kk) % 4 + 1) * D]
                    kv_norm[kk] = I("vector", lambda E, s2=xsrc, p2=p2: E.tensor_scalar(
                        out=xnb2[:, p2 * D:(p2 + 1) * D], in0=s2,
                        scalar1=mv2[:, p2 * 2:p2 * 2 + 1],
                        scalar2=rstd2[:, p2:p2 + 1],
                        op0=ALU.subtract, op1=ALU.mult), "tV")
                return f

            def u_tr(kt, r):
                def f(kt=kt, r=r):
                    kk = sup * 2 + kt
                    p2 = kk % 2
                    dst = xnkb[:].rearrange("p (k c n) -> p k c n", k=4, c=DC)[:, par * 2 + kt]
                    W("tensor", "tV", psT_free[0])
                    if r == 0:
                        W("tensor", "tV", kv_norm[kk])
                    tt = None
                    for i in range(4):
                        dch = r * 4 + i
                        tt = I("tensor", lambda E, d=dch, i=i, p2=p2: E.matmul(
                            psT[:, i * 128:(i + 1) * 128],
                            xnb2[:, p2 * D + d * 128:p2 * D + d * 128 + 128], idn[:],
                            start=(i == 0), stop=(i == 3)),
                            "tT" if i == 3 else None)
                    kt_tr[kk] = tt
                    W("vector", "tT", tt)
                    ev = I("vector", lambda E, r=r, o=dst: E.tensor_copy(
                        o[:, r * 4:(r + 1) * 4, :],
                        psT[:].rearrange("p (c n) -> p c n", c=4)), "tV")
                    psT_free[0] = ev
                return f

            for kt in range(2):
                units.append(u_dma(kt))
                units.append(u_stats(kt))
                units.append(u_rstd(kt))
                units.append(u_norm(kt))
                units.append(u_tr(kt, 0))
                units.append(u_tr(kt, 1))

            def u_k(p):
                def f(p=p):
                    W("tensor", "tV", psT_free[0])
                    tt = None
                    for g in range(2):
                        hk = 2 * p + g
                        for dch in range(DC):
                            rhs = xnkb[:].rearrange(
                                "p (e k c n) -> p c e k n", e=2, k=2, c=DC)[:, dch, par]
                            tt = I("tensor", lambda E, d=dch, hk=hk, g=g, r=rhs: E.matmul(
                                psT[:, g * 256:g * 256 + 256],
                                wkb[:, d * D + hk * 128:d * D + hk * 128 + 128],
                                r, start=(g == 0 and d == 0),
                                stop=(g == 1 and d == DC - 1)),
                                "tT" if (g == 1 and dch == DC - 1) else None)
                    W("vector", "tT", tt)
                    ev = None
                    for g in range(2):
                        hk = 2 * p + g
                        ev = I("vector", lambda E, hk=hk, g=g: E.tensor_scalar(
                            out=ktb[:, par * 2048 + hk * 256:par * 2048 + hk * 256 + 256],
                            in0=psT[:, g * 256:g * 256 + 256],
                            scalar1=bkr[:, hk:hk + 1], scalar2=None, op0=ALU.add), "tV")
                    psT_free[0] = ev
                    kv_k_ready[sup] = ev
                return f

            for p in range(4):
                units.append(u_k(p))

            def u_v(kt, half):
                def f(kt=kt, half=half):
                    W("tensor", "tV", psT_free[0])
                    mm = None
                    for dch in range(DC):
                        I("tensor", lambda E, d=dch, kt=kt, h=half: E.matmul(
                            psT[:],
                            xnkb[:, (par * 2 + kt) * D + d * 128:(par * 2 + kt) * D + d * 128 + 128],
                            wvb[:, d * D + h * 512:d * D + h * 512 + 512],
                            start=(d == 0), stop=False))
                    mm = I("tensor", lambda E, h=half: E.matmul(
                        psT[:], onesb[:], bvbo[0:1, h * 512:(h + 1) * 512],
                        start=False, stop=True), "tT")
                    W("vector", "tT", mm)
                    ev = I("vector", lambda E, kt=kt, h=half: E.tensor_copy(
                        vxb[:].rearrange("p (k h o) -> p k h o", k=4, h=H)
                        [:, par * 2 + kt, h * 8:(h + 1) * 8, 0:64],
                        psT[:].rearrange("p (h o) -> p h o", h=8)), "tV")
                    psT_free[0] = ev
                    kv_ready[sup] = ev
                return f

            for kt in range(2):
                for half in range(2):
                    units.append(u_v(kt, half))
            return units

        def emit_S(gi, sup, qt, sc, hf):
            par = sup % 2
            slot = gi % 2
            W("tensor", "tS", exp_cnt[slot])
            if (qt, sc, hf) == (0, 0, 0):
                W("tensor", "tV", kv_k_ready[sup])
            mm = None
            for i in range(8):
                h = hf * 8 + i
                po = (h % 2) * 64
                chh = h // 2
                co = (i % 2) * 512 + (i // 2) * 128
                mm = I("tensor", lambda E, sl=slot, co=co, i=i, po=po, ch=chh,
                       sc=sc, qt=qt, par=par: E.matmul(
                    psA[:, sl * 1024 + co:sl * 1024 + co + 128],
                    ktb[po:po + 64, par * 2048 + ch * 256 + sc * 128:
                        par * 2048 + ch * 256 + sc * 128 + 128],
                    qtb[po:po + 64, ch * Q + qt * 128:ch * Q + qt * 128 + 128],
                    start=(i < 2), stop=(i >= 6),
                    tile_position=(po, 0)),
                    "tT" if i == 7 else None)
            return mm

        def emit_exp(gi, smm):
            slot = gi % 2
            pslot = gi % 4
            W("scalar", "tT", smm)
            W("scalar", "tT", pv_read_done[pslot])
            e = I("scalar", lambda E, sl=slot, pl=pslot: E.activation(
                ptb[:, pl * 1024:(pl + 1) * 1024],
                psA[:, sl * 1024:(sl + 1) * 1024],
                AF.Exp, bias=constv[:, 2:3], scale=0.125), "tS")
            exp_cnt[slot] = e
            return e

        def emit_PV(gi, sup, qt, sc, hf, first_of_qt, expv):
            par = sup % 2
            slot = gi % 2
            pslot = gi % 4
            W("tensor", "tS", expv)
            if first_of_qt and prev_ctx_evac_l[0] is not None:
                W("tensor", "tV", prev_ctx_evac_l[0])
            if (qt, sc, hf) == (0, 0, 0):
                W("tensor", "tV", kv_ready[sup])
            mm = None
            for i in range(8):
                h = hf * 8 + i
                co = (h // 7) * 512 + (h % 7) * 65 if h < 14 else 1024 + (h - 14) * 65
                pcol = (i % 2) * 512 + (i // 2) * 128
                st = sc == 0 and h in (0, 7, 14)
                mm = I("tensor", lambda E, pl=pslot, pc=pcol, h=h, co=co, st=st,
                       sc=sc, par=par: E.matmul(
                    psB[:, co:co + 65],
                    ptb[:, pl * 1024 + pc:pl * 1024 + pc + 128],
                    vxb[:, (par * 2 + sc) * H * 65 + h * 65:
                        (par * 2 + sc) * H * 65 + h * 65 + 65],
                    start=st, stop=(sc == 1 and h in (6, 13, 15))),
                    "tT" if i == 7 else None)
            pv_read_done[pslot] = mm
            return mm

        def emit_ctx_evac(qt, last_pv, first_sup):
            W("vector", "tT", last_pv)
            vv = None
            for b in range(3):
                span = 455 if b < 2 else 130
                src = psB[:, b * 512:b * 512 + span]
                dstp = ctxa[:, qt * 1040 + b * 455:qt * 1040 + b * 455 + span]
                if first_sup:
                    vv = I("vector", lambda E, s=src, d=dstp: E.tensor_copy(d, s),
                           "tV" if b == 2 else None)
                else:
                    vv = I("vector", lambda E, s=src, d=dstp: E.tensor_add(d, d, s),
                           "tV" if b == 2 else None)
            return vv

        SCN = 2
        NPE = -(-48 // SCN)
        NST = NPE * EXPERTS
        pc_w = {0: None, 1: None}
        pc_r = {0: None, 1: None}

        def _pc_stage(k):
            e, sub = k // NPE, k % NPE
            hp = k % 2
            dc0 = sub * SCN
            ncn = min(SCN, 48 - dc0)
            if dc0 + ncn <= 32:
                src = d_gup[e, :, dc0:dc0 + ncn, :]
            elif dc0 >= 32:
                src = d_dwn[e].rearrange("p (c n) -> p c n", c=FC)[:, dc0 - 32:dc0 - 32 + ncn, :]
            else:
                raise AssertionError("stage straddles gu/dw boundary")
            stg = pstg[:, hp * SCN * D:hp * SCN * D + ncn * D].rearrange(
                "p (c n) -> p c n", c=ncn)
            return e, hp, dc0, ncn, src, stg

        def make_pc_read(k):
            def f(k=k):
                e, hp, dc0, ncn, src, stg = _pc_stage(k)
                if k == 0:
                    W("gpsimd", "tT", p0_tT)
                    W("gpsimd", "tV", p0_tV)
                W("gpsimd", "dsc%d" % hp, pc_w[hp])
                pc_r[hp] = I("gpsimd", lambda E, s2=src, d2=stg: E.dma_start(out=d2, in_=s2),
                             "dpc%d" % hp)
            return f

        def make_pc_write(k):
            def f(k=k):
                e, hp, dc0, ncn, src, stg = _pc_stage(k)
                W("scalar", "dpc%d" % hp, pc_r[hp])
                pc_w[hp] = I("scalar", lambda E, e=e, dc0=dc0, d2=stg, ncn=ncn: E.dma_start(
                    out=d_wsc[e, :, dc0:dc0 + ncn, :], in_=d2), "dsc%d" % hp)
            return f

        precast_units = []
        for k in range(NST):
            if k >= 2:
                precast_units.append(make_pc_write(k - 2))
            precast_units.append(make_pc_read(k))
        precast_units.append(make_pc_write(NST - 2))
        precast_units.append(make_pc_write(NST - 1))
        NPC = len(precast_units)
        pc_pos = 0

        prev_ctx_evac_l = [None]
        for u in make_kv_units(0):
            u()
        gidx = 0
        for sup in range(NS2):
            feeder = list(make_kv_units(sup + 1)) if sup + 1 < NS2 else []
            pc_want = NPC * (sup + 1) // NS2
            while pc_pos < pc_want:
                feeder.append(precast_units[pc_pos])
                pc_pos += 1
            fpos = 0
            groups = [(qt, sc, hf) for qt in range(Tq) for sc in range(2) for hf in range(2)]
            pending = []
            def _flush_pv():
                pgi, pqt, psc, phf, pexp = pending.pop(0)
                pv = emit_PV(pgi, sup, pqt, psc, phf,
                             first_of_qt=(psc == 0 and phf == 0), expv=pexp)
                if psc == 1 and phf == 1:
                    prev_ctx_evac_l[0] = emit_ctx_evac(pqt, pv, sup == 0)
                return pv
            for gli, (qt, sc, hf) in enumerate(groups):
                smm = emit_S(gidx, sup, qt, sc, hf)
                attn_last_exp = emit_exp(gidx, smm)
                if len(pending) >= 2:
                    _flush_pv()
                pending.append((gidx, qt, sc, hf, attn_last_exp))
                gidx += 1
                eff = max(1, (len(groups) * 95) // 100)
                want = min(len(feeder), (gli + 1) * len(feeder) // eff)
                while fpos < want:
                    feeder[fpos]()
                    fpos += 1
            while pending:
                pv = _flush_pv()
            qloop_pv_end[sup] = pv
            while fpos < len(feeder):
                feeder[fpos]()
                fpos += 1
        while pc_pos < NPC:
            precast_units[pc_pos]()
            pc_pos += 1
        prev_ctx_evac = prev_ctx_evac_l[0]

        attn_tT = plan.cur("tT")
        attn_tS = plan.cur("tS")
        attn_tV = plan.cur("tV")

        if PH < 2:
            return _finish(nc, plan, sems, W)
        W("gpsimd", "tT", attn_tT)
        W("gpsimd", "tS", attn_tS)
        W("gpsimd", "tV", attn_tV)
        gload(wob[:], d_wo)
        gload(gamr[:], d_gam)
        v_p15 = gload(betr[:], d_bet)

        W("tensor", "dpre", v_p15)
        W("vector", "dpre", v_p15)
        W("vector", "tT", attn_tT)
        W("scalar", "tT", attn_tT)
        ctxtr = {}
        ctxte = {}
        xtr = {}
        xnorm = {}
        wo_done = {}
        yadd_done = {}
        xuse_done = {}
        outw = {}
        xdma = {}
        p15_rstd = {}

        def p15_u1(qt):
            par = qt % 2
            W("scalar", "tV", xuse_done.get(qt - 2))
            xdma[qt] = I("scalar", lambda E, qt=qt: E.dma_start(
                out=xst2[:, (qt % 2) * D:(qt % 2 + 1) * D],
                in_=d_xo[qt * 128:(qt + 1) * 128, :]), "do%d" % (qt % 2))
            if qt == 0:
                W("vector", "tS", attn_tS)
            v = I("vector", lambda E, qt=qt, par=par: E.reciprocal(
                rden[:, par * H:(par + 1) * H],
                ctxa[:].rearrange("p (q h o) -> p q h o", q=Tq, h=H)[:, qt, :, 64]), "tV")
            W("vector", "tV", v)
            W("vector", "tT", ctxtr.get(qt - 2))
            vv = None
            for h in range(H):
                vv = I("vector", lambda E, qt=qt, h=h, par=par: E.tensor_scalar(
                    out=ctxb[:, par * D + h * 64:par * D + (h + 1) * 64],
                    in0=ctxa[:, qt * 1040 + h * 65:qt * 1040 + h * 65 + 64],
                    scalar1=rden[:, par * H + h:par * H + h + 1],
                    scalar2=None, op0=ALU.mult),
                    "tV" if h == H - 1 else None)
            ctxtr[qt], ctxte[qt] = emit_transpose8(
                ctxb[:, par * D:(par + 1) * D],
                ctxtb[:, par * D:(par + 1) * D].rearrange("p (c n) -> p c n", c=DC),
                wait_tV=vv)

        def p15_u2(qt):
            par = qt % 2
            W("tensor", "tV", ctxte[qt])
            W("tensor", "tV", yadd_done.get(qt - 2))
            if qt < 2:
                W("tensor", "tS", attn_last_exp)
            mm = None
            for half in range(2):
                for hkch in range(DC):
                    I("tensor", lambda E, hk=hkch, h=half, par=par: E.matmul(
                        psA[:, par * 1024 + h * 512:par * 1024 + (h + 1) * 512],
                        ctxtb[:, par * D + hk * 128:par * D + (hk + 1) * 128],
                        wob[:, hk * D + h * 512:hk * D + h * 512 + 512],
                        start=(hk == 0), stop=False))
                mm = I("tensor", lambda E, h=half, par=par: E.matmul(
                    psA[:, par * 1024 + h * 512:par * 1024 + (h + 1) * 512], onesb[:],
                    bvbo[0:1, D + h * 512:D + h * 512 + 512],
                    start=False, stop=True), "tT" if half == 1 else None)
            wo_done[qt] = mm
            W("vector", "do%d" % (qt % 2), xdma[qt])
            W("vector", "tS", p15_rstd.get(qt - 2))
            xsrc = xst2[:, (qt % 2) * D:(qt % 2 + 1) * D]
            I("vector", lambda E, s2=xsrc, p2=par: E.bn_stats(
                stats2[:, p2 * 12:p2 * 12 + 6], s2[:, 0:512]))
            v1 = I("vector", lambda E, s2=xsrc, p2=par: E.bn_stats(
                stats2[:, p2 * 12 + 6:p2 * 12 + 12], s2[:, 512:1024]), "tV")
            W("vector", "tV", v1)
            va = I("vector", lambda E, p2=par: E.bn_aggr(
                mv2[:, p2 * 2:p2 * 2 + 2], stats2[:, p2 * 12:p2 * 12 + 12]), "tV")
            W("scalar", "tV", va)
            W("scalar", "tV", xnorm.get(qt - 2))
            s0 = I("scalar", lambda E, p2=par: E.activation(
                lnt2[:, p2:p2 + 1], mv2[:, p2 * 2 + 1:p2 * 2 + 2],
                AF.Ln, bias=constv[:, 0:1]), "tS")
            W("scalar", "tS", s0)
            p15_rstd[qt] = I("scalar", lambda E, p2=par: E.activation(
                rstd2[:, p2:p2 + 1], lnt2[:, p2:p2 + 1], AF.Exp, scale=-0.5), "tS")
            W("vector", "tS", p15_rstd[qt])
            W("vector", "tT", xtr.get(qt - 2))
            I("vector", lambda E, s2=xsrc, p2=par: E.tensor_scalar(
                out=xnf[:, p2 * D:(p2 + 1) * D], in0=s2,
                scalar1=mv2[:, p2 * 2:p2 * 2 + 1], scalar2=rstd2[:, p2:p2 + 1],
                op0=ALU.subtract, op1=ALU.mult), "tV")
            xnorm[qt] = I("vector", lambda E, s2=xsrc, p2=par: E.tensor_scalar(
                out=xnb2[:, p2 * D:(p2 + 1) * D], in0=s2,
                scalar1=mv2[:, p2 * 2:p2 * 2 + 1], scalar2=rstd2[:, p2:p2 + 1],
                op0=ALU.subtract, op1=ALU.mult), "tV")
            dst = xntf[:].rearrange("p (c n) -> p c n", c=DC)[:, :, qt * 128:(qt + 1) * 128]
            xtr[qt], _ = emit_transpose8(
                xnb2[:, par * D:(par + 1) * D], dst, wait_tV=xnorm[qt])

        def p15_u3(qt):
            par = qt % 2
            W("vector", "tT", wo_done[qt])
            W("vector", "dow%d" % (qt % 2), outw.get(qt - 2))
            v = I("vector", lambda E, par=par: E.tensor_add(
                yf[:], xnf[:, par * D:(par + 1) * D],
                psA[:, par * 1024:par * 1024 + 1024]), "tV")
            yadd_done[qt] = v
            W("vector", "tV", v)
            sr = emit_ln_stats(yf[:], 1)
            W("vector", "tS", sr)
            yslice = yf[:]
            v = I("vector", lambda E, y=yslice: E.tensor_scalar(
                out=y, in0=y, scalar1=mv[:, 0:1], scalar2=rstd[:],
                op0=ALU.subtract, op1=ALU.mult), "tV")
            W("vector", "tV", v)
            v = I("vector", lambda E, y=yslice: E.tensor_mul(y, y, gamr[:]), "tV")
            W("vector", "tV", v)
            v = I("vector", lambda E, y=yslice: E.tensor_add(y, y, betr[:]), "tV")
            W("vector", "tV", v)
            xsrc = xst2[:, (qt % 2) * D:(qt % 2 + 1) * D]
            xuse_done[qt] = I("vector", lambda E, qt=qt, x_=xsrc, y=yslice: E.tensor_add(
                outst[:, (qt % 2) * D:(qt % 2 + 1) * D], y, x_), "tV")
            W("scalar", "tV", xuse_done[qt])
            outw[qt] = I("scalar", lambda E, qt=qt: E.dma_start(
                out=d_out[qt * 128:(qt + 1) * 128, :],
                in_=outst[:, (qt % 2) * D:(qt % 2 + 1) * D]), "dow%d" % (qt % 2))

        for i in range(Tq + 2):
            if i >= 2:
                p15_u3(i - 2)
            if i < Tq:
                p15_u1(i)
            if 1 <= i <= Tq:
                p15_u2(i - 1)

        p15_do0 = plan.cur("dow0")
        p15_do1 = plan.cur("dow1")
        p15_tV = plan.cur("tV")
        p15_tS = plan.cur("tS")

        if PH < 3:
            return _finish(nc, plan, sems, W)
        NB = max(1, TOKTH // 512)
        TBW = min(512, TOKTH)
        gu_mm_done = {}
        hid_done = {}
        hid_half = {}
        comb_done = {}
        comb_mm_last = None
        first_dw = True
        W("sync", "tV", p15_tV)
        W("sync", "tS", p15_tS)
        W("sync", "dow0", p15_do0)
        W("sync", "dow1", p15_do1)
        W("sync", "do0", plan.cur("do0"))
        W("sync", "do1", plan.cur("do1"))
        W("sync", "dsc0", plan.cur("dsc0"))
        W("sync", "dsc1", plan.cur("dsc1"))
        W("vector", "dsc0", plan.cur("dsc0"))
        W("vector", "dsc1", plan.cur("dsc1"))
        for th in range(NTH):
            W("scalar", "dow0", plan.cur("dow0"))
            W("scalar", "dow1", plan.cur("dow1"))
            W("scalar", "dsc0", plan.cur("dsc0"))
            W("scalar", "dsc1", plan.cur("dsc1"))
            vdo = I("scalar", lambda E, th=th: E.dma_start(
                out=accb[:].rearrange("p (t d) -> p t d", t=TTH),
                in_=d_out[th * TOKTH:(th + 1) * TOKTH, :]
                .rearrange("(t p) d -> p t d", p=128)), "dai")
            W("vector", "dai", vdo)
            for e in range(EXPERTS):
                for fc in range(FC):
                    buf = fc % 2
                    key_prev = (th, e, fc - 2)
                    if fc < 2:
                        if e > 0:
                            key_prev = (th, e - 1, FC - 2 + fc)
                        elif th > 0:
                            key_prev = (th - 1, EXPERTS - 1, FC - 2 + fc)
                        else:
                            key_prev = None
                    if key_prev is not None:
                        W("sync", "tT", gu_mm_done[key_prev])
                    I("sync", lambda E, e=e, fc=fc, buf=buf: E.dma_start(
                        out=wgbuf[:, buf * D:(buf + 1) * D],
                        in_=d_wsc[e, :, fc, :]), "dwg%d" % buf)
                    vg = I("sync", lambda E, e=e, fc=fc, buf=buf: E.dma_start(
                        out=wubuf[:, buf * D:(buf + 1) * D],
                        in_=d_wsc[e, :, FC + fc, :]), "dwg%d" % buf)
                    W("tensor", "dwg%d" % buf, vg)
                    if (th, e, fc) == (0, 0, 0):
                        W("tensor", "tS", p15_tS)
                        W("tensor", "tV", p15_tV)
                    key_ps = (th, e, fc - 1, 0)
                    if fc == 0:
                        if e > 0:
                            key_ps = (th, e - 1, FC - 1, 0)
                        elif th > 0:
                            key_ps = (th - 1, EXPERTS - 1, FC - 1, 0)
                        else:
                            key_ps = None
                    if key_ps is not None and key_ps in hid_half:
                        W("tensor", "tS", hid_half[key_ps][0])
                        W("tensor", "tV", hid_half[key_ps][1])
                    sl_mm = sl_silu = sl_mul = None
                    for nb in range(NB):
                        u = fc * NB + nb
                        sl = u % 2
                        kp2 = (th, e, fc - 1, 1)
                        if nb == 1 and fc == 0:
                            kp2 = (th, e - 1, FC - 1, 1) if e > 0 else (
                                (th - 1, EXPERTS - 1, FC - 1, 1) if th > 0 else None)
                        if nb == 1 and kp2 is not None and kp2 in hid_half:
                            W("tensor", "tS", hid_half[kp2][0])
                            W("tensor", "tV", hid_half[kp2][1])
                        mm = None
                        for gu in range(2):
                            wb = wgbuf if gu == 0 else wubuf
                            for dch in range(DC):
                                mm = I("tensor", lambda E, d=dch, nb=nb, th=th, buf=buf,
                                       sl=sl, gu=gu, w=wb: E.matmul(
                                    psA[:, sl * 1024 + gu * 512:sl * 1024 + gu * 512 + TBW],
                                    w[:, buf * D + d * 128:buf * D + d * 128 + 128],
                                    xntf[:, d * Q + th * TOKTH + nb * TBW:
                                         d * Q + th * TOKTH + nb * TBW + TBW],
                                    start=(d == 0), stop=(d == DC - 1)),
                                    "tT" if (gu == 1 and dch == DC - 1) else None)
                        W("scalar", "tT", mm)
                        sl_silu = I("scalar", lambda E, sl=sl: E.activation(
                            silb[:, sl * 512:sl * 512 + TBW],
                            psA[:, sl * 1024:sl * 1024 + TBW], AF.Silu), "tS")
                        W("vector", "tS", sl_silu)
                        sl_mul = I("vector", lambda E, fc=fc, nb=nb, sl=sl: E.tensor_mul(
                            hidb[:, fc * TOKTH + nb * TBW:fc * TOKTH + nb * TBW + TBW],
                            silb[:, sl * 512:sl * 512 + TBW],
                            psA[:, sl * 1024 + 512:sl * 1024 + 512 + TBW]), "tV")
                        hid_half[(th, e, fc, nb)] = (sl_silu, sl_mul)
                        sl_mm = mm
                    gu_mm_done[(th, e, fc)] = sl_mm
                    hid_done[(th, e, fc)] = (sl_silu, sl_mul)
                if first_dw:
                    first_dw = False
                else:
                    W("sync", "tT", comb_mm_last)
                vg = I("sync", lambda E, e=e: E.dma_start(
                    out=dwb[:], in_=d_wsc[e, :, 32:48, :]), "ddw")
                W("tensor", "ddw", vg)
                W("tensor", "tV", hid_done[(th, e, FC - 1)][1])
                for tt in range(TTH):
                    for dh in range(2):
                        idx = tt * 2 + dh
                        sl = idx % 2
                        key_c = (th, e, idx - 2)
                        if idx < 2:
                            if e > 0:
                                key_c = (th, e - 1, 2 * TTH - 2 + idx)
                            elif th > 0:
                                key_c = (th - 1, EXPERTS - 1, 2 * TTH - 2 + idx)
                            else:
                                key_c = None
                        if key_c is not None and key_c in comb_done:
                            W("tensor", "tV", comb_done[key_c])
                        mm = None
                        for fc in range(FC):
                            mm = I("tensor", lambda E, fc=fc, tt=tt, dh=dh, sl=sl: E.matmul(
                                psB[:, sl * 512:(sl + 1) * 512],
                                hidb[:, fc * TOKTH + tt * 128:fc * TOKTH + tt * 128 + 128],
                                dwb[:, fc * D + dh * 512:fc * D + dh * 512 + 512],
                                start=(fc == 0), stop=(fc == FC - 1)),
                                "tT" if fc == FC - 1 else None)
                        W("vector", "tT", mm)
                        gt = th * TTH + tt
                        comb_done[(th, e, idx)] = I(
                            "vector",
                            lambda E, tt=tt, dh=dh, sl=sl, gt=gt, e=e: E.scalar_tensor_tensor(
                                out=accb[:, tt * D + dh * 512:tt * D + dh * 512 + 512],
                                in0=psB[:, sl * 512:(sl + 1) * 512],
                                scalar=routw[:, gt * EXPERTS + e:gt * EXPERTS + e + 1],
                                in1=accb[:, tt * D + dh * 512:tt * D + dh * 512 + 512],
                                op0=ALU.mult, op1=ALU.add), "tV")
                comb_mm_last = plan.cur("tT")
            W("scalar", "tV", comb_done[(th, EXPERTS - 1, 2 * TTH - 1)])
            I("scalar", lambda E, th=th: E.dma_start(
                out=d_out[th * TOKTH:(th + 1) * TOKTH, :]
                .rearrange("(t p) d -> p t d", p=128),
                in_=accb[:].rearrange("p (t d) -> p t d", t=TTH)), "dow%d" % (th % 2))

        return _finish(nc, plan, sems, W)


def _finish(nc, plan, sems, W):
    for sm in DMA_SEMS:
        W("sync", sm, plan.cur(sm))
    for sm in ("tT", "tS", "tV"):
        W("sync", sm, plan.cur(sm))
    with nc.Block() as block:
        @block.sync
        def _(E):
            for fn in plan.ops["sync"]:
                fn(E)

        @block.tensor
        def _(E):
            for fn in plan.ops["tensor"]:
                fn(E)

        @block.scalar
        def _(E):
            for fn in plan.ops["scalar"]:
                fn(E)

        @block.vector
        def _(E):
            for fn in plan.ops["vector"]:
                fn(E)

        @block.gpsimd
        def _(E):
            for fn in plan.ops["gpsimd"]:
                fn(E)
    return nc


FULL_CFG = dict(S=8192, Q=2048, NTH=2)


def make_in_maps(cfg, n_cores, L0_attn_x, Wq, bq, Wk, bk, Wv, bv, Wo, bo,
                 gamma, beta, gate_w, gate_up_w, down_w):
    S, Q = cfg["S"], cfg["Q"]
    f32 = np.float32
    X = np.asarray(L0_attn_x, f32)
    def _wr(w):
        return np.ascontiguousarray(
            np.asarray(w, f32).reshape(D, D).reshape(DC, 128, D)
            .transpose(1, 0, 2).reshape(128, DC * D))
    wq, wk, wv, wo = _wr(Wq), _wr(Wk), _wr(Wv), _wr(Wo)
    bqr = np.ascontiguousarray(np.asarray(bq, f32).reshape(D).reshape(DC, 128).T)
    bkr = np.ascontiguousarray(np.asarray(bk, f32).reshape(D).reshape(DC, 128).T)
    bvbo = np.ascontiguousarray(np.concatenate(
        [np.asarray(bv, f32).reshape(D), np.asarray(bo, f32).reshape(D)])[None, :])
    gw = np.ascontiguousarray(
        np.asarray(gate_w, f32).reshape(DC, 128, EXPERTS).transpose(1, 0, 2)
        .reshape(128, DC * EXPERTS))
    gam = np.ascontiguousarray(np.broadcast_to(np.asarray(gamma, f32), (128, D)))
    bet = np.ascontiguousarray(np.broadcast_to(np.asarray(beta, f32), (128, D)))
    gup = np.ascontiguousarray(
        np.asarray(gate_up_w, f32).reshape(EXPERTS, DC, 128, 32, 128)
        .transpose(0, 2, 3, 1, 4).reshape(EXPERTS, 128, 32, DC * 128))
    dwn = np.ascontiguousarray(
        np.asarray(down_w, f32).reshape(EXPERTS, FC, 128, D)
        .transpose(0, 2, 1, 3).reshape(EXPERTS, 128, FC * D))
    ident = np.eye(128, dtype=f32)
    ones = np.ones((1, 128), f32)
    n_q = X.shape[1] // Q
    in_maps = []
    for c in range(n_cores):
        b, q = c // n_q, c % n_q
        in_maps.append(dict(
            x_batch=np.ascontiguousarray(X[b, :S]),
            x_own=np.ascontiguousarray(X[b, q * Q:(q + 1) * Q]),
            wq=wq, wk=wk, wv=wv, wo=wo,
            bq_r=bqr, bk_r=bkr, bvbo=bvbo, gatew_r=gw,
            gamma_rep=gam, beta_rep=bet, gate_up=gup, down=dwn,
            ident=ident, ones=ones))
    return in_maps


def kernel(L0_attn_x, L0_attn_Wq, L0_attn_bq, L0_attn_Wk, L0_attn_bk,
           L0_attn_Wv, L0_attn_bv, L0_attn_Wo, L0_attn_bo, L0_attn_gamma,
           L0_attn_beta, L0_ffn_gate_w, L0_ffn_gate_up_w, L0_ffn_down_w):
    cfg = FULL_CFG
    in_maps = make_in_maps(cfg, 8, L0_attn_x, L0_attn_Wq, L0_attn_bq,
                           L0_attn_Wk, L0_attn_bk, L0_attn_Wv, L0_attn_bv,
                           L0_attn_Wo, L0_attn_bo, L0_attn_gamma, L0_attn_beta,
                           L0_ffn_gate_w, L0_ffn_gate_up_w, L0_ffn_down_w)
    nc = build_program(cfg)
    res = run_bass_kernel_spmd(nc, in_maps, core_ids=list(range(8)))
    outs = [np.asarray(r["out"]) for r in res.results]
    Q = cfg["Q"]
    full = np.empty((2, 8192, D), np.float32)
    for c in range(8):
        b, q = c // 4, c % 4
        full[b, q * Q:(q + 1) * Q] = outs[c]
    return full
